# version 14
# baseline (speedup 1.0000x reference)
import numpy as np
import concourse.bass as bass
import concourse.mybir as mybir
from concourse.bass_utils import run_bass_kernel_spmd
from contextlib import ExitStack

AF = mybir.ActivationFunctionType
ALU = mybir.AluOpType
AX = mybir.AxisListType
F32 = mybir.dt.float32
BF16 = mybir.dt.bfloat16

C = 1024
I = 2816
NCB = C // 128
NIB = I // 128
SEG = 512
M32 = 12582912.0
EPS = 1e-5
RECIP_ON_GP = False


class Sem:
    def __init__(self, nc, name):
        self.h = nc.alloc_semaphore(name)
        self.v = 0


class B:
    def __init__(self, nc):
        self.nc = nc
        self.q = {k: [] for k in ("sy", "ve", "sc", "te", "gp")}

    def raw(self, qn, fn):
        self.q[qn].append(fn)

    def wait(self, qn, sem, val):
        if val and val > 0:
            self.raw(qn, lambda e, s=sem.h, v=val: e.wait_ge(s, v))

    def op(self, qn, build, sem=None, n=1):
        if sem is None:
            self.raw(qn, build)
            return None
        sem.v += n
        self.raw(qn, lambda e, b=build, s=sem.h, k=n: b(e).then_inc(s, k))
        return sem.v

    def dma(self, qn, out, in_, sem):
        return self.op(qn, lambda e, o=out, i=in_: e.dma_start(out=o, in_=i), sem, 16)

    def dma_slow(self, qn, out, in_, sem):
        return self.op(qn, lambda e, o=out, i=in_: e.dma_start(
            out=o, in_=i, allow_slow_non_contiguous=True), sem, 16)

    def cp(self, qn, out, in_, sem=None):
        if qn == "sc":
            return self.op(qn, lambda e, o=out, i=in_: e.copy(out=o, in_=i), sem)
        return self.op(qn, lambda e, o=out, i=in_: e.tensor_copy(out=o, in_=i), sem)

    def ms(self, qn, out, c, sem=None):
        return self.op(qn, lambda e, o=out, k=c: e.memset(o, k), sem)

    def tt(self, qn, out, a, b_, opc, sem=None):
        return self.op(qn, lambda e, o=out, x=a, y=b_, p=opc:
                       e.tensor_tensor(out=o, in0=x, in1=y, op=p), sem)

    def ts2(self, qn, out, a, s1, s2, op0, op1, sem=None):
        return self.op(qn, lambda e, o=out, x=a, u=s1, w=s2, p0=op0, p1=op1:
                       e.tensor_scalar(out=o, in0=x, scalar1=u, scalar2=w, op0=p0, op1=p1), sem)

    def ts1(self, qn, out, a, s1, op0, sem=None):
        return self.op(qn, lambda e, o=out, x=a, u=s1, p0=op0:
                       e.tensor_scalar(out=o, in0=x, scalar1=u, scalar2=None, op0=p0), sem)

    def stt(self, qn, out, in0, scalar, in1, op0, op1, sem=None):
        return self.op(qn, lambda e, o=out, x=in0, u=scalar, y=in1, p0=op0, p1=op1:
                       e.scalar_tensor_tensor(out=o, in0=x, scalar=u, in1=y, op0=p0, op1=p1), sem)

    def act(self, qn, out, in_, func, bias=0.0, scale=1.0, sem=None):
        return self.op(qn, lambda e, o=out, i=in_, f=func, bb=bias, ss=scale:
                       e.activation(out=o, in_=i, func=f, bias=bb, scale=ss), sem)

    def red(self, qn, out, in_, opc, absval=False, sem=None):
        return self.op(qn, lambda e, o=out, i=in_, p=opc, av=absval:
                       e.tensor_reduce(out=o, in_=i, axis=AX.X, op=p,
                                       apply_absolute_value=av), sem)

    def scan(self, qn, out, d0, d1, init, sem=None):
        return self.op(qn, lambda e, o=out, a=d0, b_=d1, i=init:
                       e.tensor_tensor_scan(out=o, data0=a, data1=b_, initial=i,
                                            op0=ALU.mult, op1=ALU.add), sem)

    def mm(self, qn, out, lhsT, rhs, start, stop, sem=None):
        return self.op(qn, lambda e, o=out, l=lhsT, r=rhs, a=start, z=stop:
                       e.matmul(out=o, lhsT=l, rhs=r, start=a, stop=z), sem)

    def tp(self, qn, out, in_, ident, sem=None):
        return self.op(qn, lambda e, o=out, i=in_, d=ident:
                       e.transpose(out=o, in_=i, identity=d), sem)

    def bn(self, qn, out, in_, sem=None):
        return self.op(qn, lambda e, o=out, i=in_: e.bn_stats(out=o, in_=i), sem)

    def bnagg(self, qn, out, in_, sem=None):
        return self.op(qn, lambda e, o=out, i=in_: e.bn_aggr(out=o, in_=i), sem)

    def recip(self, qn, out, in_, sem=None):
        return self.op(qn, lambda e, o=out, i=in_: e.reciprocal(out=o, in_=i), sem)

    def flush(self, blk):
        for qn, meth in (("sy", blk.sync), ("ve", blk.vector), ("sc", blk.scalar),
                         ("te", blk.tensor), ("gp", blk.gpsimd)):
            fns = self.q[qn]
            if fns:
                def body(e, fs=tuple(fns)):
                    for f in fs:
                        f(e)
                meth(body)
                self.q[qn] = []


def build_nc(T, debug=False):
    NSEG = T // SEG
    NTC = SEG // 128
    NCHK = T // 128
    NGRP = T // 512
    CW = 64
    NSC = SEG // CW
    nc = bass.Bass("TRN2", target_bir_lowering=False, debug=False, detect_race_conditions=False)
    bd = B(nc)

    x_d = nc.dram_tensor("x", [T, C], F32, kind="ExternalInput").ap()
    Wk_d = nc.dram_tensor("Wk", [C, C], F32, kind="ExternalInput").ap()
    Wv_d = nc.dram_tensor("Wv", [C, C], F32, kind="ExternalInput").ap()
    Wr_d = nc.dram_tensor("Wr", [C, C], F32, kind="ExternalInput").ap()
    Wo_d = nc.dram_tensor("Wo", [C, C], F32, kind="ExternalInput").ap()
    td_d = nc.dram_tensor("td", [C], F32, kind="ExternalInput").ap()
    tf_d = nc.dram_tensor("tf", [C], F32, kind="ExternalInput").ap()
    tmk_d = nc.dram_tensor("tmk", [C], F32, kind="ExternalInput").ap()
    tmv_d = nc.dram_tensor("tmv", [C], F32, kind="ExternalInput").ap()
    tmr_d = nc.dram_tensor("tmr", [C], F32, kind="ExternalInput").ap()
    wg_d = nc.dram_tensor("wg", [I, C], F32, kind="ExternalInput").ap()
    wu_d = nc.dram_tensor("wu", [I, C], F32, kind="ExternalInput").ap()
    wd_d = nc.dram_tensor("wd", [C, I], F32, kind="ExternalInput").ap()
    cx_d = nc.dram_tensor("clipx", [1], F32, kind="ExternalInput").ap()
    cd_d = nc.dram_tensor("clipd", [1], F32, kind="ExternalInput").ap()
    out_d = nc.dram_tensor("out", [T, C], F32, kind="ExternalOutput").ap()

    dkind = "ExternalOutput" if debug else "Internal"
    part_d = nc.dram_tensor("part", [T, C], F32, kind=dkind).ap()
    hsp_d = nc.dram_tensor("hsp", [I, T], F32, kind=dkind).ap()
    if debug:
        dbg_pers = nc.dram_tensor("dbg_pers", [128, 2 * (T // 128) + 16], F32,
                                  kind="ExternalOutput").ap()
        dbg_wdT = nc.dram_tensor("dbg_wdT", [128, (I // 128) * 1024], BF16,
                                 kind="ExternalOutput").ap()
        dbg_qh = nc.dram_tensor("dbg_qh", [128, 512], BF16, kind="ExternalOutput").ap()
        dbg_ps = nc.dram_tensor("dbg_ps", [128, 4096], F32, kind="ExternalOutput").ap()
        dbg_part4 = nc.dram_tensor("dbg_part4", [128, 4096], F32, kind="ExternalOutput").ap()
    mb_d = nc.dram_tensor("mb", [128], F32, kind="Internal").ap()
    wsr_d = nc.dram_tensor("wsr", [C], F32, kind="Internal").ap()
    cc1i = nc.dram_tensor("cc1i", [1, 128], F32, kind="Internal")
    cc1o = nc.dram_tensor("cc1o", [1, 128], F32, kind="Internal")
    cc2i = nc.dram_tensor("cc2i", [1, 128], F32, kind="Internal")
    cc2o = nc.dram_tensor("cc2o", [1, 128], F32, kind="Internal")

    s_ld = Sem(nc, "s_ld")
    s_st = Sem(nc, "s_st")
    s_ve = Sem(nc, "s_ve")
    s_sc = Sem(nc, "s_sc")
    s_te = Sem(nc, "s_te")
    s_gp = Sem(nc, "s_gp")
    s_c1 = Sem(nc, "s_c1")
    s_c2 = Sem(nc, "s_c2")
    s_lx = Sem(nc, "s_lx")
    s_lf = [Sem(nc, "s_lf0"), Sem(nc, "s_lf1")]
    s_lp = [Sem(nc, "s_lp0"), Sem(nc, "s_lp1")]
    s_lh2 = [Sem(nc, "s_lh0"), Sem(nc, "s_lh1"), Sem(nc, "s_lh2"), Sem(nc, "s_lh3")]

    A = nc.alloc_sbuf_tensor
    stats_b = A("stats_b", [128, NCHK], F32)
    stats_r = A("stats_r", [128, NCHK], F32)
    runmax = A("runmax", [128, 1], F32)
    runmaxh = A("runmaxh", [128, 1], F32)
    tmax = A("tmax", [128, 1], F32)
    identbf = A("identbf", [128, 128], BF16)
    identfp = A("identfp", [128, 128], F32)
    maxpad = A("maxpad", [128, 128], F32)
    onesf = A("onesf", [1, 128], F32)
    mrow = A("mrow", [1, 128], F32)
    gmax1 = A("gmax1", [1, 1], F32)
    gmax2 = A("gmax2", [1, 1], F32)
    cxt = A("cxt", [1, 1], F32)
    cdt = A("cdt", [1, 1], F32)
    sgt = A("sgt", [1, 4], F32)
    sxb = A("sxb", [128, 1], F32)
    isxb = A("isxb", [128, 1], F32)
    shb = A("shb", [128, 1], F32)
    ishb = A("ishb", [128, 1], F32)
    z128 = A("z128", [1, 128], F32)
    m32b = A("m32b", [128, 1], F32)
    tiny = A("tinyb", [1, 4], F32)
    wtmp = A("wtmp", [128, 16], F32)

    def vpad(n=1):
        for _ in range(n):
            bd.cp("ve", tiny[:, 0:1], tiny[:, 1:2])

    marks = {}

    blk_cm = nc.Block()
    blk = blk_cm.__enter__()

    with ExitStack() as SC1:
        def SA(name, shape, dt=F32):
            return SC1.enter_context(nc.sbuf_tensor(name, shape, dt))

        def SP(name, shape, dt=F32):
            return SC1.enter_context(nc.psum_tensor(name, shape, dt))

        iotaA = SA("iotaA", [128, 128])
        td_t = SA("td_t", [128, NCB])
        tf_t = SA("tf_t", [128, NCB])
        tmk_t = SA("tmk_t", [128, NCB])
        tmv_t = SA("tmv_t", [128, NCB])
        tmr_t = SA("tmr_t", [128, NCB])
        alpha_t = SA("alpha_t", [128, NCB])
        eu_t = SA("eu_t", [128, NCB])
        e1_t = SA("e1_t", [128, NCB])
        e2_t = SA("e2_t", [128, NCB])
        ab = SA("ab", [128, NCB * 128])
        apow = SA("apow", [128, NCB * 128])
        a128 = SA("a128", [128, NCB * NSC])
        zer128 = SA("zer128", [128, 128])
        wkT = SA("wkT", [128, NCB * 1024], BF16)
        wvT = SA("wvT", [128, NCB * 1024], BF16)
        wrT = SA("wrT", [128, NCB * 1024], BF16)
        woT = SA("woT", [128, NCB * 1024], BF16)

        for vec_d, tile in ((td_d, td_t), (tf_d, tf_t), (tmk_d, tmk_t),
                            (tmv_d, tmv_t), (tmr_d, tmr_t)):
            v3 = vec_d.rearrange("(b p one) -> b p one", p=128, one=1)
            for b_ in range(NCB):
                bd.dma("sy", tile[:, b_:b_ + 1], v3[b_], s_ld)
        bd.dma("sy", cxt[0:1, :], cx_d.rearrange("(a b) -> a b", a=1), s_ld)
        bd.dma("sy", cdt[0:1, :], cd_d.rearrange("(a b) -> a b", a=1), s_ld)
        ld_vecs = s_ld.v

        bd.op("gp", lambda e: e.iota(iotaA[:], pattern=[[1, 128]], base=0,
                                     channel_multiplier=-1,
                                     allow_small_or_imprecise_dtypes=True), s_gp)
        bd.wait("ve", s_gp, s_gp.v)
        bd.ts1("ve", identfp[:], iotaA[:], 0.0, ALU.is_equal)
        bd.cp("ve", identbf[:], identfp[:])
        bd.ms("ve", maxpad[:], 0.0)
        bd.ms("ve", onesf[:], 1.0)
        bd.ms("ve", z128[:], 0.0)
        bd.ms("ve", zer128[:], 0.0)
        bd.ms("ve", tiny[:], 0.0)
        bd.ms("ve", m32b[:], M32)
        bd.ms("ve", sgt[:], 0.0)
        bd.ms("ve", runmax[:], 0.0)
        bd.ms("ve", runmaxh[:], 0.0, sem=s_ve)
        ve_const = s_ve.v
        bd.wait("sy", s_ve, ve_const)
        bd.dma("sy", cc1i.ap(), z128[:], s_ld)
        bd.dma("sy", cc2i.ap(), z128[:], s_ld)

        bd.wait("sc", s_ld, ld_vecs)
        bd.act("sc", e1_t[:], td_t[:], AF.Exp)
        bd.act("sc", eu_t[:], tf_t[:], AF.Exp, sem=s_sc)
        bd.wait("ve", s_sc, s_sc.v)
        bd.ts1("ve", e2_t[:], e1_t[:], -1.0, ALU.mult, sem=s_ve)
        bd.wait("sc", s_ve, s_ve.v)
        bd.act("sc", alpha_t[:], e2_t[:], AF.Exp, sem=s_sc)
        bd.wait("ve", s_sc, s_sc.v)
        for j in range(NCB):
            sl = slice(j * 128, (j + 1) * 128)
            bd.ms("ve", ab[:, sl], 1.0)
            bd.ts1("ve", ab[:, sl], ab[:, sl], alpha_t[:, j:j + 1], ALU.mult)
        for j in range(NCB):
            bd.scan("ve", apow[:, j * 128:j * 128 + 64], ab[:, j * 128:j * 128 + 64],
                    zer128[:, 0:64], 1.0)
        for j in range(NCB):
            bd.scan("ve", apow[:, j * 128 + 64:(j + 1) * 128],
                    ab[:, j * 128:j * 128 + 64], zer128[:, 0:64],
                    apow[:, j * 128 + 63:j * 128 + 64])
        for j in range(NCB):
            s4 = slice(j * NSC, (j + 1) * NSC)
            bd.ms("ve", a128[:, s4], 1.0)
            bd.ts1("ve", a128[:, s4], a128[:, s4],
                   apow[:, j * 128 + CW - 1:j * 128 + CW], ALU.mult)

        with nc.sbuf_tensor("wstg", [128, 2 * 1024], BF16) as wstg, \
             nc.psum_tensor("ps_w", [128, 2048], BF16) as ps_w:
            bd.wait("te", s_ve, ve_const)
            tpms, evms = [], []
            g_i = [0]

            def wprep(w_src, dstT):
                for rb in range(NCB):
                    g = g_i[0]
                    buf = wstg[:, (g % 2) * 1024:((g % 2) + 1) * 1024]
                    pss = ps_w[:, (g % 2) * 1024:((g % 2) + 1) * 1024]
                    if g >= 2:
                        bd.wait("gp", s_te, tpms[g - 2])
                    bd.dma("gp", buf, w_src[rb * 128:(rb + 1) * 128, :], s_st)
                    bd.wait("te", s_st, s_st.v)
                    if g >= 2:
                        bd.wait("te", s_sc, evms[g - 2])
                    for cb in range(NCB):
                        bd.tp("te", pss[:, cb * 128:(cb + 1) * 128],
                              buf[:, cb * 128:(cb + 1) * 128], identbf[:],
                              sem=s_te if cb == NCB - 1 else None)
                    tpms.append(s_te.v)
                    bd.wait("sc", s_te, s_te.v)
                    for cb in range(NCB):
                        bd.cp("sc", dstT[:, cb * 1024 + rb * 128:cb * 1024 + (rb + 1) * 128],
                              pss[:, cb * 128:(cb + 1) * 128],
                              sem=s_sc if cb == NCB - 1 else None)
                    evms.append(s_sc.v)
                    g_i[0] += 1

            wprep(Wk_d, wkT)
            wprep(Wv_d, wvT)
            wprep(Wr_d, wrT)
            wprep(Wo_d, woT)
            marks["wprep_te"] = s_te.v
            bd.flush(blk)

        x_seg = SA("x_seg", [128, 2 * NTC * 1024])
        xn_sb = SA("xn_sb", [128, 1024], BF16)
        xn_fp = SA("xn_fp", [128, 1024])
        bnraw = SA("bnraw", [128, 24])
        mvs = SA("mvs", [128, 8])
        var4 = SA("var4", [128, 4])
        negm4 = SA("negm4", [128, 4])
        nr = SA("nr", [128, 4])
        ntp = SA("ntp", [128, 4])
        xnT = [SA(f"xnT{j}", [128, SEG + 1], BF16) for j in range(NCB)]
        dmix = SA("dmix", [128, SEG], BF16)
        xkT = [SA(f"xkT{j}", [128, SEG], BF16) for j in range(NCB)]
        xvT = [SA(f"xvT{j}", [128, SEG], BF16) for j in range(NCB)]
        xrT = [SA(f"xrT{j}", [128, SEG], BF16) for j in range(NCB)]
        Ek = [SA(f"Ek{p}", [128, SEG]) for p in range(2)]
        vT = [SA(f"vT{p}", [128, SEG]) for p in range(2)]
        er = [SA(f"er{p}", [128, SEG]) for p in range(2)]
        ekv = SA("ekv", [128, SEG])
        Asc = SA("Asc", [128, SEG + 2])
        Bsc = SA("Bsc", [128, SEG + 2])
        endsA = SA("endsA", [128, NSC])
        endsB = SA("endsB", [128, NSC])
        carrA = SA("carrA", [128, NSC])
        carrB = SA("carrB", [128, NSC])
        segcA = SA("segcA", [128, NCB])
        segcB = SA("segcB", [128, NCB])
        numt = [SA("numt0", [128, SEG]), SA("numt1", [128, SEG])]
        dent = SA("dent", [128, SEG])
        rect = SA("rect", [128, 2 * SEG])
        rwkvT = [SA(f"rwkvT{j}", [128, SEG], BF16) for j in range(NCB)]
        po_sb = [SA("po_sb0", [128, 1024]), SA("po_sb1", [128, 1024])]

        ps_tp = SP("ps_tp", [128, 1024], BF16)
        ps_k = SP("ps_k", [128, SEG])
        ps_v = SP("ps_v", [128, SEG])
        ps_r = SP("ps_r", [128, SEG])
        ps_o = SP("ps_o", [128, 1024])

        bd.ms("ve", segcA[:], 0.0)
        bd.ms("ve", segcB[:], 0.0)
        for j in range(NCB):
            bd.ms("ve", xnT[j][:, 0:1], 0.0)
        if RECIP_ON_GP:
            negone = SA("negone", [128, SEG])
            bd.ms("ve", negone[:], -1.0)

        xload = {}
        stats_mk = {}
        xn_mk = {}
        absm_mk = {}
        tpm_mk = {}
        evac_mk = {}
        mix_mk = {}
        kvr_mk = {}
        exps_mk = {}
        exps_k_mk = {}
        exps_v_mk = {}
        ekv_mk = {}
        scanv_mk = {}
        rwkv_mk = {}
        rect_mk = {}
        oproj_mk = {}
        po_mk = {}
        post_mk = {}
        runmax_final = None

        def emit_a5(s):
            bd.wait("te", s_ve, rwkv_mk[s])
            for tc in range(NTC):
                pk = (s, tc - 1) if tc >= 1 else ((s - 1, NTC - 1) if s >= 1 else None)
                if pk is not None:
                    bd.wait("te", s_sc, po_mk[pk])
                for half in range(2):
                    for j in range(NCB):
                        bd.mm("te", ps_o[:, half * 512:(half + 1) * 512],
                              rwkvT[j][:, tc * 128:(tc + 1) * 128],
                              woT[:, j * 1024 + half * 512:j * 1024 + (half + 1) * 512],
                              j == 0, j == NCB - 1,
                              sem=s_te if (half == 1 and j == NCB - 1) else None)
                oproj_mk[s] = s_te.v
                bd.wait("sc", s_te, oproj_mk[s])
                pk2 = (s, tc - 2) if tc >= 2 else ((s - 1, tc + 2) if s >= 1 else None)
                if pk2 is not None:
                    bd.wait("sc", s_st, post_mk[pk2])
                bd.cp("sc", po_sb[tc % 2][:], ps_o[:], sem=s_sc)
                po_mk[(s, tc)] = s_sc.v
                gc = s * NTC + tc
                bd.wait("gp", s_sc, po_mk[(s, tc)])
                bd.dma("gp", part_d[gc * 128:(gc + 1) * 128, :], po_sb[tc % 2][:], s_st)
                post_mk[(s, tc)] = s_st.v


        def xs_ap(s, m):
            return x_seg[:, ((s % 2) * NTC + m) * 1024:((s % 2) * NTC + m + 1) * 1024]

        def emit_loads(s):
            for m in range(NTC):
                gc = s * NTC + m
                if s == 0 and m == 0:
                    bd.wait("sy", s_te, marks["wprep_te"])
                if s >= 2 and m == 0:
                    bd.wait("sy", s_sc, xn_mk[(s - 1) * NTC - 1])
                bd.dma("sy", xs_ap(s, m), x_d[gc * 128:(gc + 1) * 128, :], s_lx)
                xload[gc] = s_lx.v

        def emit_stats(s):
            for mp in range(NTC // 2):
                m0, m1 = 2 * mp, 2 * mp + 1
                xs0 = xs_ap(s, m0)
                xs1 = xs_ap(s, m1)
                bd.wait("ve", s_lx, xload[s * NTC + NTC - 1])
                bd.bn("ve", bnraw[:, 0:6], xs0[:, 0:512])
                bd.bn("ve", bnraw[:, 6:12], xs0[:, 512:1024])
                bd.bn("ve", bnraw[:, 12:18], xs1[:, 0:512])
                bd.bn("ve", bnraw[:, 18:24], xs1[:, 512:1024])
                bd.bnagg("ve", mvs[:, 2 * m0:2 * m0 + 2], bnraw[:, 0:12])
                bd.bnagg("ve", mvs[:, 2 * m1:2 * m1 + 2], bnraw[:, 12:24])
            vpad(1)
            ga = var4[:, 0:2]
            gb_ = var4[:, 2:4]
            bd.cp("ve", ga[:, 0:1], mvs[:, 1:2])
            bd.cp("ve", ga[:, 1:2], mvs[:, 3:4])
            bd.cp("ve", gb_[:, 0:1], mvs[:, 5:6])
            bd.cp("ve", gb_[:, 1:2], mvs[:, 7:8])
            bd.ts1("ve", ga, ga, EPS, ALU.add)
            bd.ts1("ve", gb_, gb_, EPS, ALU.add)
            bd.ts1("ve", negm4[:, 0:1], mvs[:, 0:1], -1.0, ALU.mult)
            bd.ts1("ve", negm4[:, 1:2], mvs[:, 2:3], -1.0, ALU.mult)
            bd.ts1("ve", negm4[:, 2:3], mvs[:, 4:5], -1.0, ALU.mult)
            bd.ts1("ve", negm4[:, 3:4], mvs[:, 6:7], -1.0, ALU.mult)
            bd.ts2("ve", nr[:, 0:2], ga, -0.5, 1.5, ALU.mult, ALU.add)
            bd.ts2("ve", nr[:, 2:4], gb_, -0.5, 1.5, ALU.mult, ALU.add)
            for it in range(4):
                bd.tt("ve", ntp[:, 0:2], nr[:, 0:2], nr[:, 0:2], ALU.mult)
                bd.tt("ve", ntp[:, 2:4], nr[:, 2:4], nr[:, 2:4], ALU.mult)
                bd.tt("ve", ntp[:, 0:2], ntp[:, 0:2], ga, ALU.mult)
                bd.tt("ve", ntp[:, 2:4], ntp[:, 2:4], gb_, ALU.mult)
                bd.ts2("ve", ntp[:, 0:2], ntp[:, 0:2], -0.5, 1.5, ALU.mult, ALU.add)
                bd.ts2("ve", ntp[:, 2:4], ntp[:, 2:4], -0.5, 1.5, ALU.mult, ALU.add)
                bd.tt("ve", nr[:, 0:2], nr[:, 0:2], ntp[:, 0:2], ALU.mult)
                bd.tt("ve", nr[:, 2:4], nr[:, 2:4], ntp[:, 2:4], ALU.mult)
            vpad(1)
            bd.cp("ve", stats_r[:, s * NTC:(s + 1) * NTC], nr[:, 0:4])
            vpad(1)
            bd.tt("ve", stats_b[:, s * NTC:(s + 1) * NTC], negm4[:], nr[:, 0:4],
                  ALU.mult, sem=s_ve)
            stats_mk[s] = s_ve.v

        def emit_a1b(s):
            nonlocal runmax_final
            for m in range(NTC):
                gc = s * NTC + m
                xs = xs_ap(s, m)
                bd.wait("sc", s_ve, stats_mk[s])
                if gc >= 1:
                    bd.wait("sc", s_te, tpm_mk[gc - 1])
                    bd.wait("sc", s_ve, absm_mk[gc - 1])
                bd.act("sc", xn_fp[:], xs, AF.Identity,
                       bias=stats_b[:, gc:gc + 1], scale=stats_r[:, gc:gc + 1])
                bd.act("sc", xn_sb[:], xs, AF.Identity,
                       bias=stats_b[:, gc:gc + 1], scale=stats_r[:, gc:gc + 1], sem=s_sc)
                xn_mk[gc] = s_sc.v
                bd.wait("ve", s_sc, xn_mk[gc])
                bd.red("ve", tmax[:], xn_fp[:], ALU.max, absval=True)
                vpad(1)
                bd.tt("ve", runmax[:], runmax[:], tmax[:], ALU.max, sem=s_ve)
                absm_mk[gc] = s_ve.v
                bd.wait("te", s_sc, xn_mk[gc])
                if gc >= 1:
                    bd.wait("te", s_sc, evac_mk[gc - 1])
                for j in range(NCB):
                    bd.tp("te", ps_tp[:, j * 128:(j + 1) * 128],
                          xn_sb[:, j * 128:(j + 1) * 128], identbf[:],
                          sem=s_te if j == NCB - 1 else None)
                tpm_mk[gc] = s_te.v
                bd.wait("sc", s_te, tpm_mk[gc])
                if m == 0 and s >= 1:
                    bd.wait("sc", s_ve, mix_mk[s - 1])
                for j in range(NCB):
                    bd.cp("sc", xnT[j][:, 1 + m * 128:1 + (m + 1) * 128],
                          ps_tp[:, j * 128:(j + 1) * 128],
                          sem=s_sc if j == NCB - 1 else None)
                evac_mk[gc] = s_sc.v
            if s == NSEG - 1:
                runmax_final = absm_mk[s * NTC + NTC - 1]

        def emit_mixes(s):
            bd.wait("ve", s_sc, evac_mk[s * NTC + NTC - 1])
            if s >= 1:
                bd.wait("ve", s_te, kvr_mk[(s - 1, NCB - 1)])
            for j in range(NCB):
                cur = xnT[j][:, 1:SEG + 1]
                prev = xnT[j][:, 0:SEG]
                bd.tt("ve", dmix[:], cur, prev, ALU.subtract)
                bd.stt("ve", xkT[j][:], dmix[:], tmk_t[:, j:j + 1], prev, ALU.mult, ALU.add)
                bd.stt("ve", xvT[j][:], dmix[:], tmv_t[:, j:j + 1], prev, ALU.mult, ALU.add)
                bd.stt("ve", xrT[j][:], dmix[:], tmr_t[:, j:j + 1], prev, ALU.mult, ALU.add)
                bd.cp("ve", xnT[j][:, 0:1], xnT[j][:, SEG:SEG + 1],
                      sem=s_ve if j == NCB - 1 else None)
            mix_mk[s] = s_ve.v

        emit_loads(0)
        emit_stats(0)
        for s in range(NSEG):
            if s >= 1:
                emit_stats(s)
            if s + 1 < NSEG:
                emit_loads(s + 1)
            emit_a1b(s)
            emit_mixes(s)
            if s >= 1:
                emit_a5(s - 1)

            for j in range(NCB):
                pj = j % 2
                bd.wait("te", s_ve, mix_mk[s])
                pkey = (s, j - 1) if j >= 1 else ((s - 1, NCB - 1) if s >= 1 else None)
                if pkey is not None:
                    bd.wait("te", s_sc, exps_k_mk[pkey])
                for cb in range(NCB):
                    bd.mm("te", ps_k[:], wkT[:, cb * 1024 + j * 128:cb * 1024 + (j + 1) * 128],
                          xkT[cb][:], cb == 0, cb == NCB - 1)
                if pkey is not None:
                    bd.wait("te", s_sc, exps_v_mk[pkey])
                for cb in range(NCB):
                    bd.mm("te", ps_v[:], wvT[:, cb * 1024 + j * 128:cb * 1024 + (j + 1) * 128],
                          xvT[cb][:], cb == 0, cb == NCB - 1)
                if pkey is not None:
                    bd.wait("te", s_sc, exps_mk[pkey])
                for cb in range(NCB):
                    bd.mm("te", ps_r[:], wrT[:, cb * 1024 + j * 128:cb * 1024 + (j + 1) * 128],
                          xrT[cb][:], cb == 0, cb == NCB - 1,
                          sem=s_te if cb == NCB - 1 else None)
                kvr_mk[(s, j)] = s_te.v

                bd.wait("sc", s_te, kvr_mk[(s, j)])
                gidx = s * NCB + j
                if gidx >= 2:
                    pk2 = (s, j - 2) if j >= 2 else (s - 1, NCB - 2 + j)
                    bd.wait("sc", s_ve, scanv_mk[pk2])
                bd.act("sc", Ek[pj][:], ps_k[:], AF.Exp, sem=s_sc)
                exps_k_mk[(s, j)] = s_sc.v
                bd.act("sc", vT[pj][:], ps_v[:], AF.Identity, sem=s_sc)
                exps_v_mk[(s, j)] = s_sc.v
                bd.act("sc", er[pj][:], ps_r[:], AF.Exp, scale=-1.0, sem=s_sc)
                exps_mk[(s, j)] = s_sc.v

                bd.wait("ve", s_sc, exps_mk[(s, j)])
                bd.tt("ve", ekv[:], Ek[pj][:], vT[pj][:], ALU.mult)
                bd.cp("ve", Asc[:, 1:2], segcA[:, j:j + 1])
                bd.cp("ve", Bsc[:, 1:2], segcB[:, j:j + 1])
                abj = ab[:, j * 128:j * 128 + CW]
                for ch in range(NSC):
                    bd.scan("ve", Asc[:, 2 + ch * CW:2 + (ch + 1) * CW], abj,
                            ekv[:, ch * CW:(ch + 1) * CW], 0.0)
                    bd.scan("ve", Bsc[:, 2 + ch * CW:2 + (ch + 1) * CW], abj,
                            Ek[pj][:, ch * CW:(ch + 1) * CW], 0.0)
                for ch in range(NSC):
                    bd.cp("ve", endsA[:, ch:ch + 1],
                          Asc[:, 1 + (ch + 1) * CW:2 + (ch + 1) * CW])
                    bd.cp("ve", endsB[:, ch:ch + 1],
                          Bsc[:, 1 + (ch + 1) * CW:2 + (ch + 1) * CW])
                a128j = a128[:, j * NSC:(j + 1) * NSC]
                bd.scan("ve", carrA[:], a128j, endsA[:], segcA[:, j:j + 1])
                bd.scan("ve", carrB[:], a128j, endsB[:], segcB[:, j:j + 1])
                apj = apow[:, j * 128:j * 128 + CW]
                for ch in range(NSC):
                    scalA = segcA[:, j:j + 1] if ch == 0 else carrA[:, ch - 1:ch]
                    scalB = segcB[:, j:j + 1] if ch == 0 else carrB[:, ch - 1:ch]
                    sl = slice(2 + ch * CW, 2 + (ch + 1) * CW)
                    bd.stt("ve", Asc[:, sl], apj, scalA, Asc[:, sl], ALU.mult, ALU.add)
                    bd.stt("ve", Bsc[:, sl], apj, scalB, Bsc[:, sl], ALU.mult, ALU.add)
                bd.cp("ve", segcA[:, j:j + 1], carrA[:, NSC - 1:NSC])
                bd.cp("ve", segcB[:, j:j + 1], carrB[:, NSC - 1:NSC])
                bd.stt("ve", numt[pj][:], ekv[:], eu_t[:, j:j + 1], Asc[:, 1:SEG + 1],
                       ALU.mult, ALU.add)
                if j >= 1:
                    bd.wait("ve", s_sc, rect_mk[(s, j - 1)])
                    if j == 1 and s >= 1:
                        bd.wait("ve", s_te, oproj_mk[s - 1])
                    jm = j - 1
                    bd.tt("ve", rwkvT[jm][:], numt[1 - pj][:],
                          rect[:, (jm % 2) * SEG:(jm % 2 + 1) * SEG], ALU.mult)
                bd.stt("ve", dent[:], Ek[pj][:], eu_t[:, j:j + 1], Bsc[:, 1:SEG + 1],
                       ALU.mult, ALU.add)
                bd.stt("ve", dent[:], er[pj][:], 1.0, dent[:], ALU.add, ALU.mult, sem=s_ve)
                scanv_mk[(s, j)] = s_ve.v
                bd.wait("sc", s_ve, scanv_mk[(s, j)])
                rsl = rect[:, (j % 2) * SEG:(j % 2 + 1) * SEG]
                bd.act("sc", rsl, dent[:], AF.Ln)
                bd.act("sc", rsl, rsl, AF.Exp, scale=-1.0, sem=s_sc)
                rect_mk[(s, j)] = s_sc.v
                if j == NCB - 1:
                    bd.wait("ve", s_sc, rect_mk[(s, j)])
                    bd.tt("ve", rwkvT[j][:], numt[pj][:], rsl, ALU.mult, sem=s_ve)
            rwkv_mk[s] = s_ve.v

        emit_a5(NSEG - 1)
        marks["attn_ve"] = s_ve.v
        marks["attn_sc"] = s_sc.v
        marks["attn_te"] = s_te.v
        marks["part_st"] = s_st.v

        bd.wait("ve", s_ve, runmax_final)
        bd.cp("ve", maxpad[:, 0:1], runmax[:], sem=s_ve)
        mp_ve = s_ve.v
        with nc.psum_tensor("ps_bc", [128, 128]) as ps_bc:
            bd.wait("te", s_ve, mp_ve)
            bd.tp("te", ps_bc[:], maxpad[:], identfp[:], sem=s_te)
            tp1_te = s_te.v
            bd.wait("ve", s_te, tp1_te)
            bd.red("ve", gmax1[0:1, :], ps_bc[0:1, :], ALU.max, sem=s_ve)
            bd.wait("sy", s_ve, s_ve.v)
            bd.dma("sy", cc1i.ap()[0:1, 0:1], gmax1[0:1, :], s_ld)
            bd.wait("gp", s_ld, s_ld.v)
            bd.op("gp", lambda e: e.collective_compute(
                "AllReduce", ALU.max, replica_groups=[[0, 1, 2, 3, 4, 5, 6, 7]],
                ins=[cc1i.ap().opt()], outs=[cc1o.ap().opt()]), s_c1)
            bd.wait("sy", s_c1, 1)
            bd.dma("sy", gmax1[0:1, :], cc1o.ap()[0:1, 0:1], s_ld)
            gmax1_ld = s_ld.v
            bd.wait("sc", s_ld, gmax1_ld)
            bd.act("sc", sgt[0:1, 0:1], cxt[0:1, :], AF.Exp, scale=-1.0, sem=s_sc)
            bd.wait("ve", s_sc, s_sc.v)
            bd.ts1("ve", sgt[0:1, 1:2], sgt[0:1, 0:1], 1.0, ALU.add)
            vpad(1)
            bd.recip("ve", sgt[0:1, 1:2], sgt[0:1, 1:2])
            vpad(1)
            bd.tt("ve", sgt[0:1, 2:3], sgt[0:1, 1:2], gmax1[0:1, :], ALU.mult)
            vpad(1)
            bd.ts1("ve", sgt[0:1, 3:4], sgt[0:1, 2:3], 1.0 / 127.0, ALU.mult)
            vpad(1)
            bd.recip("ve", sgt[0:1, 0:1], sgt[0:1, 3:4], sem=s_ve)
            sx_ve = s_ve.v
            bd.wait("te", s_ve, sx_ve)
            bd.mm("te", ps_bc[:, 0:1], onesf[0:1, :], sgt[0:1, 3:4], True, True)
            bd.mm("te", ps_bc[:, 1:2], onesf[0:1, :], sgt[0:1, 0:1], True, True, sem=s_te)
            bd.wait("ve", s_te, s_te.v)
            bd.cp("ve", sxb[:], ps_bc[:, 0:1])
            bd.cp("ve", isxb[:], ps_bc[:, 1:2])
            vpad(2)
            bd.cp("ve", tiny[:, 2:3], tiny[:, 3:4], sem=s_ve)
            marks["bc1_ve"] = s_ve.v
            bd.flush(blk)

    with ExitStack() as SC2:
        def SA2(name, shape, dt=F32):
            return SC2.enter_context(nc.sbuf_tensor(name, shape, dt))

        wgT = SA2("wgT", [128, NCB * I], BF16)
        wuT = SA2("wuT", [128, NCB * I], BF16)
        wg_rs = SA2("wg_rs", [128, NIB])
        wu_rs = SA2("wu_rs", [128, NIB])
        gsr = SA2("gsr", [128, NIB])
        hgsr = SA2("hgsr", [128, NIB])
        usr = SA2("usr", [128, NIB])
        wfst = [SA2(f"wfst{p}", [128, C]) for p in range(4)]
        qst = [SA2(f"qst{p}", [128, C], BF16) for p in range(4)]
        xf = [SA2(f"xf{p}", [128, 1024]) for p in range(2)]
        rs_isx = SA2("rs_isx", [128, NCHK])
        bs_isx = SA2("bs_isx", [128, NCHK])
        qxtm = SA2("qxtm", [128, 1024])
        qxbf = SA2("qxbf", [128, 1024], BF16)
        qxT = [SA2(f"qxT{cb}", [128, 512], BF16) for cb in range(NCB)]
        thh = SA2("thh", [128, 1024])
        ggg = SA2("ggg", [128, 1024])
        uuu = SA2("uuu", [128, 1024])
        guu = SA2("guu", [128, 512])
        hsb = [SA2(f"hsb{p}", [128, 512]) for p in range(2)]
        hmax2 = SA2("hmax2", [128, 2])

        bd.wait("sy", s_ve, marks["attn_ve"])
        bd.wait("sy", s_sc, marks["attn_sc"])
        bd.wait("sy", s_te, marks["attn_te"])

        wq_tp_mk = []
        wq_ev_mk = []
        wq_q_mk = []
        with nc.psum_tensor("ps_wq", [128, 4096], BF16) as ps_wq:
            for ib in range(NIB):
                par2 = ib % 2
                if ib >= 2:
                    bd.wait("sy", s_ve, wq_q_mk[ib - 2])
                bd.dma("sy", wfst[2 * par2][:], wg_d[ib * 128:(ib + 1) * 128, :], s_ld)
                bd.dma("sy", wfst[2 * par2 + 1][:], wu_d[ib * 128:(ib + 1) * 128, :], s_ld)
                wld = s_ld.v
                bd.wait("ve", s_ld, wld)
                if ib >= 2:
                    bd.wait("ve", s_te, wq_tp_mk[ib - 2])
                w0 = wfst[2 * par2]
                w1 = wfst[2 * par2 + 1]
                q0 = qst[2 * par2]
                q1 = qst[2 * par2 + 1]
                c4 = 8 * par2
                bd.red("ve", wtmp[:, c4 + 0:c4 + 1], w0[:], ALU.max, absval=True)
                bd.red("ve", wtmp[:, c4 + 1:c4 + 2], w1[:], ALU.max, absval=True)
                bd.recip("ve", wtmp[:, c4 + 2:c4 + 3], wtmp[:, c4 + 0:c4 + 1])
                bd.recip("ve", wtmp[:, c4 + 3:c4 + 4], wtmp[:, c4 + 1:c4 + 2])
                bd.ts1("ve", wtmp[:, c4 + 4:c4 + 5], wtmp[:, c4 + 2:c4 + 3], 127.0, ALU.mult)
                bd.ts1("ve", wtmp[:, c4 + 5:c4 + 6], wtmp[:, c4 + 3:c4 + 4], 127.0, ALU.mult)
                bd.ts1("ve", wg_rs[:, ib:ib + 1], wtmp[:, c4 + 0:c4 + 1], 1.0 / 127.0, ALU.mult)
                bd.ts1("ve", wu_rs[:, ib:ib + 1], wtmp[:, c4 + 1:c4 + 2], 1.0 / 127.0, ALU.mult)
                for k, (ww, qq) in enumerate(((w0, q0), (w1, q1))):
                    bd.ts2("ve", ww[:], ww[:], wtmp[:, c4 + 4 + k:c4 + 5 + k], M32,
                           ALU.mult, ALU.add)
                    bd.ts2("ve", ww[:], ww[:], M32 + 127.0, M32 - 128.0,
                           ALU.min, ALU.max)
                    bd.ts1("ve", qq[:], ww[:], M32, ALU.subtract,
                           sem=s_ve if k == 1 else None)
                wq_q_mk.append(s_ve.v)
                bd.wait("te", s_ve, wq_q_mk[-1])
                if ib >= 2:
                    bd.wait("te", s_sc, wq_ev_mk[ib - 2])
                for k, qq in ((0, q0), (1, q1)):
                    for cb in range(NCB):
                        bd.tp("te", ps_wq[:, par2 * 2048 + k * 1024 + cb * 128:
                                          par2 * 2048 + k * 1024 + (cb + 1) * 128],
                              qq[:, cb * 128:(cb + 1) * 128], identbf[:],
                              sem=s_te if (k == 1 and cb == NCB - 1) else None)
                wq_tp_mk.append(s_te.v)
                bd.wait("sc", s_te, wq_tp_mk[-1])
                for k, dstT in ((0, wgT), (1, wuT)):
                    for cb in range(NCB):
                        bd.cp("sc", dstT[:, cb * I + ib * 128:cb * I + (ib + 1) * 128],
                              ps_wq[:, par2 * 2048 + k * 1024 + cb * 128:
                                       par2 * 2048 + k * 1024 + (cb + 1) * 128],
                              sem=s_sc if (k == 1 and cb == NCB - 1) else None)
                wq_ev_mk.append(s_sc.v)
            bd.flush(blk)

        ps_qx = SC2.enter_context(nc.psum_tensor("ps_qx", [128, 1024], BF16))
        ps_g = SC2.enter_context(nc.psum_tensor("ps_g", [128, 1024]))
        ps_u = SC2.enter_context(nc.psum_tensor("ps_u", [128, 1024]))
        bd.wait("ve", s_ve, marks["bc1_ve"])
        bd.ts1("ve", gsr[:], wg_rs[:], sxb[:, 0:1], ALU.mult)
        bd.ts1("ve", usr[:], wu_rs[:], sxb[:, 0:1], ALU.mult)
        bd.ts1("ve", hgsr[:], gsr[:], 0.5, ALU.mult)
        bd.ts1("ve", usr[:], usr[:], 0.5, ALU.mult)
        bd.ts1("ve", rs_isx[:], stats_r[:], isxb[:, 0:1], ALU.mult)
        bd.ts1("ve", bs_isx[:], stats_b[:], isxb[:, 0:1], ALU.mult, sem=s_ve)
        scl_ve = s_ve.v

        hstore_mk = {}
        f1q_mk = {}
        f1tp_mk = {}
        f1ev_mk = {}
        f1gu_mk = {}
        f1sc_mk = {}
        f1h_mk = {}
        for grp in range(NGRP):
            for m4 in range(4):
                gc = grp * 4 + m4
                p = gc % 2
                if gc >= 2:
                    bd.wait("sy", s_ve, f1q_mk[gc - 2])
                bd.dma("sy", xf[p][:], x_d[gc * 128:(gc + 1) * 128, :], s_lf[p])
                bd.wait("sc", s_lf[p], s_lf[p].v)
                if gc == 0:
                    bd.wait("sc", s_ve, scl_ve)
                if gc >= 1:
                    bd.wait("sc", s_ve, f1q_mk[gc - 1])
                bd.act("sc", qxtm[:], xf[p][:], AF.Identity,
                       bias=bs_isx[:, gc:gc + 1], scale=rs_isx[:, gc:gc + 1], sem=s_sc)
                xnf_mk = s_sc.v
                bd.wait("ve", s_sc, xnf_mk)
                if gc >= 1:
                    bd.wait("ve", s_te, f1tp_mk[gc - 1])
                bd.ts2("ve", qxtm[:], qxtm[:], M32, M32 + 127.0, ALU.add, ALU.min)
                bd.ts2("ve", qxbf[:], qxtm[:], M32 - 128.0, M32, ALU.max, ALU.subtract,
                       sem=s_ve)
                f1q_mk[gc] = s_ve.v
                bd.wait("te", s_ve, f1q_mk[gc])
                if gc >= 1:
                    bd.wait("te", s_sc, f1ev_mk[gc - 1])
                for cb in range(NCB):
                    bd.tp("te", ps_qx[:, cb * 128:(cb + 1) * 128],
                          qxbf[:, cb * 128:(cb + 1) * 128], identbf[:],
                          sem=s_te if cb == NCB - 1 else None)
                f1tp_mk[gc] = s_te.v
                bd.wait("sc", s_te, f1tp_mk[gc])
                if m4 == 0 and grp >= 1:
                    bd.wait("sc", s_te, f1gu_mk[(grp - 1, NIB - 1)])
                for cb in range(NCB):
                    bd.cp("sc", qxT[cb][:, m4 * 128:(m4 + 1) * 128],
                          ps_qx[:, cb * 128:(cb + 1) * 128],
                          sem=s_sc if cb == NCB - 1 else None)
                f1ev_mk[gc] = s_sc.v
            for ib in range(NIB):
                p = ib % 2
                gi = grp * NIB + ib
                psg = ps_g[:, p * 512:(p + 1) * 512]
                psu = ps_u[:, p * 512:(p + 1) * 512]
                bd.wait("te", s_sc, f1ev_mk[grp * 4 + 3])
                if gi >= 2:
                    pk = (grp, ib - 2) if ib >= 2 else (grp - 1, NIB - 2 + ib)
                    bd.wait("te", s_sc, f1sc_mk[pk])
                for cb in range(NCB):
                    bd.mm("te", psg, wgT[:, cb * I + ib * 128:cb * I + (ib + 1) * 128],
                          qxT[cb][:], cb == 0, cb == NCB - 1)
                for cb in range(NCB):
                    bd.mm("te", psu, wuT[:, cb * I + ib * 128:cb * I + (ib + 1) * 128],
                          qxT[cb][:], cb == 0, cb == NCB - 1,
                          sem=s_te if cb == NCB - 1 else None)
                f1gu_mk[(grp, ib)] = s_te.v
                bd.wait("sc", s_te, f1gu_mk[(grp, ib)])
                if gi >= 2:
                    pk = (grp, ib - 2) if ib >= 2 else (grp - 1, NIB - 2 + ib)
                    bd.wait("sc", s_ve, f1h_mk[pk])
                th2 = thh[:, p * 512:(p + 1) * 512]
                gg2 = ggg[:, p * 512:(p + 1) * 512]
                uu2 = uuu[:, p * 512:(p + 1) * 512]
                bd.act("sc", th2, psg, AF.Tanh, scale=hgsr[:, ib:ib + 1])
                bd.act("sc", gg2, psg, AF.Identity, scale=gsr[:, ib:ib + 1])
                bd.act("sc", uu2, psu, AF.Identity, scale=usr[:, ib:ib + 1], sem=s_sc)
                f1sc_mk[(grp, ib)] = s_sc.v
                bd.wait("ve", s_sc, f1sc_mk[(grp, ib)])
                if gi >= 2:
                    pk = (grp, ib - 2) if ib >= 2 else (grp - 1, NIB - 2 + ib)
                    bd.wait("ve", s_st, hstore_mk[pk])
                bd.tt("ve", guu[:], gg2, uu2, ALU.mult)
                bd.stt("ve", hsb[p][:], th2, 1.0, guu[:], ALU.add, ALU.mult)
                bd.red("ve", hmax2[:, p:p + 1], hsb[p][:], ALU.max, absval=True, sem=s_ve)
                f1h_mk[(grp, ib)] = s_ve.v
                if gi >= 1:
                    pp = 1 - p
                    bd.tt("ve", runmaxh[:], runmaxh[:], hmax2[:, pp:pp + 1], ALU.max)
                bd.wait("gp", s_ve, f1h_mk[(grp, ib)])
                bd.dma("gp", hsp_d[ib * 128:(ib + 1) * 128, grp * 512:(grp + 1) * 512],
                       hsb[p][:], s_st)
                hstore_mk[(grp, ib)] = s_st.v
        lastp = ((NGRP - 1) * NIB + NIB - 1) % 2
        vpad(1)
        bd.tt("ve", runmaxh[:], runmaxh[:], hmax2[:, lastp:lastp + 1], ALU.max, sem=s_ve)
        runmaxh_final = s_ve.v
        marks["f1_st"] = s_st.v
        marks["f1_ve"] = s_ve.v
        marks["f1_sc"] = s_sc.v
        marks["f1_te"] = s_te.v

        bd.wait("ve", s_ve, runmaxh_final)
        bd.cp("ve", maxpad[:, 0:1], runmaxh[:], sem=s_ve)
        mp2_ve = s_ve.v
        with nc.psum_tensor("ps_mx2", [128, 128]) as ps_mx2:
            bd.wait("te", s_ve, mp2_ve)
            bd.wait("te", s_sc, marks["f1_sc"])
            bd.tp("te", ps_mx2[:], maxpad[:], identfp[:], sem=s_te)
            bd.wait("ve", s_te, s_te.v)
            bd.red("ve", gmax2[0:1, :], ps_mx2[0:1, :], ALU.max, sem=s_ve)
            bd.wait("sy", s_ve, s_ve.v)
            bd.dma("sy", cc2i.ap()[0:1, 0:1], gmax2[0:1, :], s_ld)
            bd.wait("gp", s_ld, s_ld.v)
            bd.op("gp", lambda e: e.collective_compute(
                "AllReduce", ALU.max, replica_groups=[[0, 1, 2, 3, 4, 5, 6, 7]],
                ins=[cc2i.ap().opt()], outs=[cc2o.ap().opt()]), s_c2)
            bd.flush(blk)

    with ExitStack() as SC3:
        def SA3(name, shape, dt=F32):
            return SC3.enter_context(nc.sbuf_tensor(name, shape, dt))

        wdT = SA3("wdT", [128, NIB * 1024], BF16)
        wd_rs = SA3("wd_rs", [128, NCB])
        hl = [SA3(f"hl{p}", [128, 512]) for p in range(4)]
        xf2 = [SA3(f"xf2{p}", [128, 1024]) for p in range(2)]
        qh = [SA3(f"qh{p}", [128, 512], BF16) for p in range(4)]
        part_sb = [SA3(f"part_sb{p}", [128, 4096]) for p in range(2)]
        outs_sb = [SA3(f"outs_sb{p}", [128, 1024]) for p in range(2)]
        wdst = [SA3(f"wdst{p}", [128, I]) for p in range(2)]
        qdst = [SA3(f"qdst{p}", [128, I], BF16) for p in range(2)]

        with nc.psum_tensor("ps_wd", [128, 2048], BF16) as ps_wd:
            bd.wait("sy", s_ve, marks["f1_ve"])
            bd.wait("sy", s_sc, marks["f1_sc"])
            bd.wait("sy", s_te, marks["f1_te"])
            wd_q_mk = []
            wd_tp_last = []
            wd_ev_last = []
            for cbp in range(NCB // 2):
                c0, c1 = 2 * cbp, 2 * cbp + 1
                if cbp >= 1:
                    bd.wait("sy", s_ve, wd_q_mk[-1])
                bd.dma("sy", wdst[0][:], wd_d[c0 * 128:(c0 + 1) * 128, :], s_ld)
                bd.dma("sy", wdst[1][:], wd_d[c1 * 128:(c1 + 1) * 128, :], s_ld)
                bd.wait("ve", s_ld, s_ld.v)
                if cbp >= 1:
                    bd.wait("ve", s_te, wd_tp_last[-1])
                bd.red("ve", wtmp[:, 0:1], wdst[0][:], ALU.max, absval=True)
                bd.red("ve", wtmp[:, 1:2], wdst[1][:], ALU.max, absval=True)
                bd.recip("ve", wtmp[:, 2:3], wtmp[:, 0:1])
                bd.recip("ve", wtmp[:, 3:4], wtmp[:, 1:2])
                bd.ts1("ve", wtmp[:, 4:5], wtmp[:, 2:3], 127.0, ALU.mult)
                bd.ts1("ve", wtmp[:, 5:6], wtmp[:, 3:4], 127.0, ALU.mult)
                bd.ts1("ve", wd_rs[:, c0:c0 + 1], wtmp[:, 0:1], 1.0 / 127.0, ALU.mult)
                bd.ts1("ve", wd_rs[:, c1:c1 + 1], wtmp[:, 1:2], 1.0 / 127.0, ALU.mult)
                for k, cc in ((0, c0), (1, c1)):
                    bd.ts2("ve", wdst[k][:], wdst[k][:], wtmp[:, 4 + k:5 + k], M32,
                           ALU.mult, ALU.add)
                    bd.ts2("ve", wdst[k][:], wdst[k][:], M32 + 127.0, M32 - 128.0,
                           ALU.min, ALU.max)
                    bd.ts1("ve", wdst[k][:], wdst[k][:], M32, ALU.subtract)
                    bd.ts1("ve", qdst[k][:], wdst[k][:], wd_rs[:, cc:cc + 1], ALU.mult,
                           sem=s_ve if k == 1 else None)
                wd_q_mk.append(s_ve.v)
                bd.wait("te", s_ve, wd_q_mk[-1])
                for k, cc in ((0, c0), (1, c1)):
                    for ig in range((NIB + 7) // 8):
                        n_in_g = min(8, NIB - ig * 8)
                        if wd_ev_last:
                            bd.wait("te", s_sc, wd_ev_last[-1])
                        for t in range(n_in_g):
                            ib = ig * 8 + t
                            bd.tp("te", ps_wd[:, k * 1024 + t * 128:k * 1024 + (t + 1) * 128],
                                  qdst[k][:, ib * 128:(ib + 1) * 128], identbf[:],
                                  sem=s_te if t == n_in_g - 1 else None)
                        wd_tp_last.append(s_te.v)
                        bd.wait("sc", s_te, wd_tp_last[-1])
                        for t in range(n_in_g):
                            ib = ig * 8 + t
                            bd.cp("sc", wdT[:, ib * 1024 + cc * 128:ib * 1024 + (cc + 1) * 128],
                                  ps_wd[:, k * 1024 + t * 128:k * 1024 + (t + 1) * 128],
                                  sem=s_sc if t == n_in_g - 1 else None)
                        wd_ev_last.append(s_sc.v)
            wd_done_sc = s_sc.v
            bd.flush(blk)

        bd.wait("sy", s_c2, 1)
        bd.dma("sy", gmax2[0:1, :], cc2o.ap()[0:1, 0:1], s_ld)
        gmax2_ld = s_ld.v
        bd.wait("sc", s_ld, gmax2_ld)
        bd.act("sc", sgt[0:1, 0:1], cdt[0:1, :], AF.Exp, scale=-1.0, sem=s_sc)
        bd.wait("ve", s_sc, s_sc.v)
        bd.ts1("ve", sgt[0:1, 1:2], sgt[0:1, 0:1], 1.0, ALU.add)
        vpad(1)
        bd.recip("ve", sgt[0:1, 1:2], sgt[0:1, 1:2])
        vpad(1)
        bd.tt("ve", sgt[0:1, 2:3], sgt[0:1, 1:2], gmax2[0:1, :], ALU.mult)
        vpad(1)
        bd.ts1("ve", sgt[0:1, 3:4], sgt[0:1, 2:3], 1.0 / 127.0, ALU.mult)
        vpad(1)
        bd.recip("ve", sgt[0:1, 0:1], sgt[0:1, 3:4], sem=s_ve)
        sh_ve = s_ve.v
        with nc.psum_tensor("ps_bc2", [128, 2]) as ps_bc2:
            bd.wait("te", s_ve, sh_ve)
            bd.mm("te", ps_bc2[:, 0:1], onesf[0:1, :], sgt[0:1, 3:4], True, True)
            bd.mm("te", ps_bc2[:, 1:2], onesf[0:1, :], sgt[0:1, 0:1], True, True, sem=s_te)
            bd.wait("ve", s_te, s_te.v)
            bd.cp("ve", shb[:], ps_bc2[:, 0:1])
            bd.cp("ve", ishb[:], ps_bc2[:, 1:2])
            vpad(2)
            bd.cp("ve", tiny[:, 2:3], tiny[:, 3:4], sem=s_ve)
            bc2_ve = s_ve.v
            bd.flush(blk)
        wsd_ve = bc2_ve

        ps_d = SC3.enter_context(nc.psum_tensor("ps_d", [128, 4096]))
        f2q_mk = {}
        f2sc_mk = {}
        f2te_mk = {}
        f2ev_mk = {}
        f2st_mk = {}
        bd.wait("sy", s_st, marks["part_st"])
        bd.wait("sy", s_st, marks["f1_st"])
        for grp in range(NGRP):
            for m4 in range(4):
                gc = grp * 4 + m4
                if grp >= 2 and m4 == 0:
                    bd.wait("sy", s_ve, f2ev_mk[(grp - 2, 3)])
                bd.dma("sy", part_sb[grp % 2][:, m4 * 1024:(m4 + 1) * 1024],
                       part_d[gc * 128:(gc + 1) * 128, :], s_lp[grp % 2])
            part_ld = s_lp[grp % 2].v
            for ib in range(NIB):
                p = ib % 4
                gi = grp * NIB + ib
                if gi >= 4:
                    pk4 = (grp, ib - 4) if ib >= 4 else (grp - 1, NIB - 4 + ib)
                    bd.wait("sy", s_ve, f2q_mk[pk4])
                bd.dma("sy", hl[p][:], hsp_d[ib * 128:(ib + 1) * 128,
                                             grp * 512:(grp + 1) * 512], s_lh2[p])
                hld = s_lh2[p].v
                bd.wait("sc", s_lh2[p], hld)
                if gi == 0:
                    bd.wait("sc", s_ve, wsd_ve)
                bd.act("sc", hl[p][:], hl[p][:], AF.Identity,
                       bias=m32b[:, 0:1], scale=ishb[:, 0:1], sem=s_sc)
                f2sc_mk[(grp, ib)] = s_sc.v
                bd.wait("ve", s_sc, f2sc_mk[(grp, ib)])
                if gi >= 2:
                    pk = (grp, ib - 2) if ib >= 2 else (grp - 1, NIB - 2 + ib)
                    bd.wait("ve", s_te, f2te_mk[pk])
                bd.ts2("ve", hl[p][:], hl[p][:], M32 + 127.0, M32 - 128.0, ALU.min, ALU.max)
                bd.ts1("ve", qh[p][:], hl[p][:], M32, ALU.subtract, sem=s_ve)
                f2q_mk[(grp, ib)] = s_ve.v
                if debug and grp == 4 and ib == 0:
                    qh_cap = SC3.enter_context(nc.sbuf_tensor("qh_cap", [128, 512], BF16))
                    bd.cp("ve", qh_cap[:], qh[p][:], sem=s_ve)
                    bd.wait("gp", s_ve, s_ve.v)
                    bd.dma("gp", dbg_qh, qh_cap[:], s_st)
                bd.wait("te", s_ve, f2q_mk[(grp, ib)])
                if gi == 0:
                    bd.wait("te", s_sc, wd_done_sc)
                if ib == 0 and grp >= 1:
                    bd.wait("te", s_ve, f2ev_mk[(grp - 1, 3)])
                for tc in range(4):
                    for half in range(2):
                        bd.mm("te", ps_d[:, tc * 1024 + half * 512:tc * 1024 + (half + 1) * 512],
                              qh[p][:, tc * 128:(tc + 1) * 128],
                              wdT[:, ib * 1024 + half * 512:ib * 1024 + (half + 1) * 512],
                              ib == 0, ib == NIB - 1,
                              sem=s_te if (tc == 3 and half == 1) else None)
                f2te_mk[(grp, ib)] = s_te.v
            bd.wait("ve", s_te, f2te_mk[(grp, NIB - 1)])
            bd.wait("ve", s_lp[grp % 2], part_ld)
            if debug and grp == 4:
                psc = SC3.enter_context(nc.sbuf_tensor("psc", [128, 4096], F32))
                bd.cp("ve", psc[:, 0:2048], ps_d[:, 0:2048])
                bd.cp("ve", psc[:, 2048:4096], ps_d[:, 2048:4096], sem=s_ve)
                bd.wait("gp", s_ve, s_ve.v)
                bd.dma("gp", dbg_ps, psc[:], s_st)
                bd.dma("gp", dbg_part4, part_sb[grp % 2][:], s_st)
            for tc in range(4):
                gc = grp * 4 + tc
                if grp * 4 + tc >= 2:
                    pgc = grp * 4 + tc - 2
                    bd.wait("sy", s_ve, f2ev_mk[(pgc // 4, pgc % 4)])
                bd.dma("sy", xf2[tc % 2][:], x_d[gc * 128:(gc + 1) * 128, :],
                       s_lf[tc % 2])
                xf2_ld = s_lf[tc % 2].v
                pk = (grp, tc - 2) if tc >= 2 else ((grp - 1, tc + 2) if grp >= 1 else None)
                if pk is not None:
                    bd.wait("ve", s_st, f2st_mk[pk])
                bd.wait("ve", s_lf[tc % 2], xf2_ld)
                for half in range(2):
                    sl = slice(tc * 1024 + half * 512, tc * 1024 + (half + 1) * 512)
                    hsl = slice(half * 512, (half + 1) * 512)
                    bd.ts1("ve", outs_sb[tc % 2][:, hsl], ps_d[:, sl], shb[:, 0:1], ALU.mult)
                    bd.tt("ve", outs_sb[tc % 2][:, hsl], outs_sb[tc % 2][:, hsl],
                          part_sb[grp % 2][:, tc * 1024 + half * 512:
                                           tc * 1024 + (half + 1) * 512], ALU.add)
                    bd.tt("ve", outs_sb[tc % 2][:, hsl], outs_sb[tc % 2][:, hsl],
                          xf2[tc % 2][:, hsl], ALU.add,
                          sem=s_ve if half == 1 else None)
                f2ev_mk[(grp, tc)] = s_ve.v
                bd.wait("gp", s_ve, f2ev_mk[(grp, tc)])
                bd.dma("gp", out_d[gc * 128:(gc + 1) * 128, :], outs_sb[tc % 2][:], s_st)
                f2st_mk[(grp, tc)] = s_st.v

        if debug:
            NCHKd = T // 128
            dstg = SC3.enter_context(nc.sbuf_tensor("dstg", [128, 2 * NCHKd + 16], F32))
            bd.wait("ve", s_sc, s_sc.v)
            bd.ms("ve", dstg[:], 0.0)
            bd.cp("ve", dstg[:, 0:NCHKd], stats_b[:])
            bd.cp("ve", dstg[:, NCHKd:2 * NCHKd], stats_r[:])
            bd.cp("ve", dstg[0:1, 2 * NCHKd + 0:2 * NCHKd + 1], gmax1[0:1, :])
            bd.cp("ve", dstg[0:1, 2 * NCHKd + 1:2 * NCHKd + 2], gmax2[0:1, :])
            bd.cp("ve", dstg[:, 2 * NCHKd + 2:2 * NCHKd + 3], sxb[:])
            bd.cp("ve", dstg[:, 2 * NCHKd + 3:2 * NCHKd + 4], isxb[:])
            bd.cp("ve", dstg[:, 2 * NCHKd + 4:2 * NCHKd + 5], shb[:])
            bd.cp("ve", dstg[:, 2 * NCHKd + 5:2 * NCHKd + 6], ishb[:])
            bd.cp("ve", dstg[:, 2 * NCHKd + 6:2 * NCHKd + 7], runmax[:])
            bd.cp("ve", dstg[:, 2 * NCHKd + 7:2 * NCHKd + 8], runmaxh[:], sem=s_ve)
            bd.wait("gp", s_ve, s_ve.v)
            bd.dma("gp", dbg_pers[:], dstg[:], s_st)
            bd.dma("gp", dbg_wdT, wdT[:], s_st)
        bd.wait("gp", s_st, s_st.v)
        bd.flush(blk)

    blk_cm.__exit__(None, None, None)
    return nc


_NC_CACHE = {}


def kernel(L0_attn_x, L0_attn_time_decay, L0_attn_time_first, L0_attn_time_mix_k,
           L0_attn_time_mix_v, L0_attn_time_mix_r, L0_attn_Wk, L0_attn_Wv, L0_attn_Wr,
           L0_attn_Wo, L0_ffn_clip_x, L0_ffn_clip_down, L0_ffn_w_gate, L0_ffn_w_up,
           L0_ffn_w_down):
    x = np.ascontiguousarray(np.asarray(L0_attn_x, dtype=np.float32))
    Bn, T, Cc = x.shape
    if T not in _NC_CACHE:
        _NC_CACHE[T] = build_nc(T)
    nc = _NC_CACHE[T]

    def f32(a):
        return np.ascontiguousarray(np.asarray(a, dtype=np.float32))

    shared = {
        "Wk": f32(L0_attn_Wk), "Wv": f32(L0_attn_Wv), "Wr": f32(L0_attn_Wr),
        "Wo": f32(L0_attn_Wo), "td": f32(L0_attn_time_decay),
        "tf": f32(L0_attn_time_first), "tmk": f32(L0_attn_time_mix_k),
        "tmv": f32(L0_attn_time_mix_v), "tmr": f32(L0_attn_time_mix_r),
        "wg": f32(L0_ffn_w_gate), "wu": f32(L0_ffn_w_up), "wd": f32(L0_ffn_w_down),
        "clipx": f32(L0_ffn_clip_x), "clipd": f32(L0_ffn_clip_down),
    }
    in_maps = [dict(shared, x=x[c]) for c in range(Bn)]
    res = run_bass_kernel_spmd(nc, in_maps, core_ids=list(range(Bn)))
    out = np.stack([np.asarray(res.results[c]["out"]) for c in range(Bn)], axis=0)
    return out.astype(np.float32)


# version 15
# speedup vs baseline: 1.0711x; 1.0711x over previous
import numpy as np
import concourse.bass as bass
import concourse.mybir as mybir
from concourse.bass_utils import run_bass_kernel_spmd
from contextlib import ExitStack

AF = mybir.ActivationFunctionType
ALU = mybir.AluOpType
AX = mybir.AxisListType
F32 = mybir.dt.float32
BF16 = mybir.dt.bfloat16

C = 1024
I = 2816
NCB = C // 128
NIB = I // 128
SEG = 512
M32 = 12582912.0
EPS = 1e-5
RECIP_ON_GP = False


class Sem:
    def __init__(self, nc, name):
        self.h = nc.alloc_semaphore(name)
        self.v = 0


class B:
    def __init__(self, nc):
        self.nc = nc
        self.q = {k: [] for k in ("sy", "ve", "sc", "te", "gp")}

    def raw(self, qn, fn):
        self.q[qn].append(fn)

    def wait(self, qn, sem, val):
        if val and val > 0:
            self.raw(qn, lambda e, s=sem.h, v=val: e.wait_ge(s, v))

    def op(self, qn, build, sem=None, n=1):
        if sem is None:
            self.raw(qn, build)
            return None
        sem.v += n
        self.raw(qn, lambda e, b=build, s=sem.h, k=n: b(e).then_inc(s, k))
        return sem.v

    def dma(self, qn, out, in_, sem):
        return self.op(qn, lambda e, o=out, i=in_: e.dma_start(out=o, in_=i), sem, 16)

    def dma_slow(self, qn, out, in_, sem):
        return self.op(qn, lambda e, o=out, i=in_: e.dma_start(
            out=o, in_=i, allow_slow_non_contiguous=True), sem, 16)

    def cp(self, qn, out, in_, sem=None):
        if qn == "sc":
            return self.op(qn, lambda e, o=out, i=in_: e.copy(out=o, in_=i), sem)
        return self.op(qn, lambda e, o=out, i=in_: e.tensor_copy(out=o, in_=i), sem)

    def ms(self, qn, out, c, sem=None):
        return self.op(qn, lambda e, o=out, k=c: e.memset(o, k), sem)

    def tt(self, qn, out, a, b_, opc, sem=None):
        return self.op(qn, lambda e, o=out, x=a, y=b_, p=opc:
                       e.tensor_tensor(out=o, in0=x, in1=y, op=p), sem)

    def ts2(self, qn, out, a, s1, s2, op0, op1, sem=None):
        return self.op(qn, lambda e, o=out, x=a, u=s1, w=s2, p0=op0, p1=op1:
                       e.tensor_scalar(out=o, in0=x, scalar1=u, scalar2=w, op0=p0, op1=p1), sem)

    def ts1(self, qn, out, a, s1, op0, sem=None):
        return self.op(qn, lambda e, o=out, x=a, u=s1, p0=op0:
                       e.tensor_scalar(out=o, in0=x, scalar1=u, scalar2=None, op0=p0), sem)

    def stt(self, qn, out, in0, scalar, in1, op0, op1, sem=None):
        return self.op(qn, lambda e, o=out, x=in0, u=scalar, y=in1, p0=op0, p1=op1:
                       e.scalar_tensor_tensor(out=o, in0=x, scalar=u, in1=y, op0=p0, op1=p1), sem)

    def act(self, qn, out, in_, func, bias=0.0, scale=1.0, sem=None):
        return self.op(qn, lambda e, o=out, i=in_, f=func, bb=bias, ss=scale:
                       e.activation(out=o, in_=i, func=f, bias=bb, scale=ss), sem)

    def red(self, qn, out, in_, opc, absval=False, sem=None):
        return self.op(qn, lambda e, o=out, i=in_, p=opc, av=absval:
                       e.tensor_reduce(out=o, in_=i, axis=AX.X, op=p,
                                       apply_absolute_value=av), sem)

    def scan(self, qn, out, d0, d1, init, sem=None):
        return self.op(qn, lambda e, o=out, a=d0, b_=d1, i=init:
                       e.tensor_tensor_scan(out=o, data0=a, data1=b_, initial=i,
                                            op0=ALU.mult, op1=ALU.add), sem)

    def mm(self, qn, out, lhsT, rhs, start, stop, sem=None):
        return self.op(qn, lambda e, o=out, l=lhsT, r=rhs, a=start, z=stop:
                       e.matmul(out=o, lhsT=l, rhs=r, start=a, stop=z), sem)

    def tp(self, qn, out, in_, ident, sem=None):
        return self.op(qn, lambda e, o=out, i=in_, d=ident:
                       e.transpose(out=o, in_=i, identity=d), sem)

    def bn(self, qn, out, in_, sem=None):
        return self.op(qn, lambda e, o=out, i=in_: e.bn_stats(out=o, in_=i), sem)

    def bnagg(self, qn, out, in_, sem=None):
        return self.op(qn, lambda e, o=out, i=in_: e.bn_aggr(out=o, in_=i), sem)

    def recip(self, qn, out, in_, sem=None):
        return self.op(qn, lambda e, o=out, i=in_: e.reciprocal(out=o, in_=i), sem)

    def flush(self, blk):
        for qn, meth in (("sy", blk.sync), ("ve", blk.vector), ("sc", blk.scalar),
                         ("te", blk.tensor), ("gp", blk.gpsimd)):
            fns = self.q[qn]
            if fns:
                def body(e, fs=tuple(fns)):
                    for f in fs:
                        f(e)
                meth(body)
                self.q[qn] = []


def build_nc(T, debug=False):
    NSEG = T // SEG
    NTC = SEG // 128
    NCHK = T // 128
    NGRP = T // 512
    CW = 64
    NSC = SEG // CW
    nc = bass.Bass("TRN2", target_bir_lowering=False, debug=False, detect_race_conditions=False)
    bd = B(nc)

    x_d = nc.dram_tensor("x", [T, C], F32, kind="ExternalInput").ap()
    Wk_d = nc.dram_tensor("Wk", [C, C], F32, kind="ExternalInput").ap()
    Wv_d = nc.dram_tensor("Wv", [C, C], F32, kind="ExternalInput").ap()
    Wr_d = nc.dram_tensor("Wr", [C, C], F32, kind="ExternalInput").ap()
    Wo_d = nc.dram_tensor("Wo", [C, C], F32, kind="ExternalInput").ap()
    td_d = nc.dram_tensor("td", [C], F32, kind="ExternalInput").ap()
    tf_d = nc.dram_tensor("tf", [C], F32, kind="ExternalInput").ap()
    tmk_d = nc.dram_tensor("tmk", [C], F32, kind="ExternalInput").ap()
    tmv_d = nc.dram_tensor("tmv", [C], F32, kind="ExternalInput").ap()
    tmr_d = nc.dram_tensor("tmr", [C], F32, kind="ExternalInput").ap()
    wg_d = nc.dram_tensor("wg", [I, C], F32, kind="ExternalInput").ap()
    wu_d = nc.dram_tensor("wu", [I, C], F32, kind="ExternalInput").ap()
    wd_d = nc.dram_tensor("wd", [C, I], F32, kind="ExternalInput").ap()
    cx_d = nc.dram_tensor("clipx", [1], F32, kind="ExternalInput").ap()
    cd_d = nc.dram_tensor("clipd", [1], F32, kind="ExternalInput").ap()
    out_d = nc.dram_tensor("out", [T, C], F32, kind="ExternalOutput").ap()

    dkind = "ExternalOutput" if debug else "Internal"
    part_d = nc.dram_tensor("part", [T, C], F32, kind=dkind).ap()
    hsp_d = nc.dram_tensor("hsp", [I, T], F32, kind=dkind).ap()
    if debug:
        dbg_pers = nc.dram_tensor("dbg_pers", [128, 2 * (T // 128) + 16], F32,
                                  kind="ExternalOutput").ap()
        dbg_wdT = nc.dram_tensor("dbg_wdT", [128, (I // 128) * 1024], BF16,
                                 kind="ExternalOutput").ap()
        dbg_qh = nc.dram_tensor("dbg_qh", [128, 512], BF16, kind="ExternalOutput").ap()
        dbg_ps = nc.dram_tensor("dbg_ps", [128, 4096], F32, kind="ExternalOutput").ap()
        dbg_part4 = nc.dram_tensor("dbg_part4", [128, 4096], F32, kind="ExternalOutput").ap()
    mb_d = nc.dram_tensor("mb", [128], F32, kind="Internal").ap()
    wsr_d = nc.dram_tensor("wsr", [C], F32, kind="Internal").ap()
    cc1i = nc.dram_tensor("cc1i", [1, 128], F32, kind="Internal")
    cc1o = nc.dram_tensor("cc1o", [1, 128], F32, kind="Internal")
    cc2i = nc.dram_tensor("cc2i", [1, 128], F32, kind="Internal")
    cc2o = nc.dram_tensor("cc2o", [1, 128], F32, kind="Internal")

    s_ld = Sem(nc, "s_ld")
    s_st = Sem(nc, "s_st")
    s_ve = Sem(nc, "s_ve")
    s_sc = Sem(nc, "s_sc")
    s_te = Sem(nc, "s_te")
    s_gp = Sem(nc, "s_gp")
    s_c1 = Sem(nc, "s_c1")
    s_c2 = Sem(nc, "s_c2")
    s_lx = Sem(nc, "s_lx")
    s_lf = [Sem(nc, "s_lf0"), Sem(nc, "s_lf1")]
    s_lp = [Sem(nc, "s_lp0"), Sem(nc, "s_lp1")]
    s_lh2 = [Sem(nc, "s_lh0"), Sem(nc, "s_lh1"), Sem(nc, "s_lh2"), Sem(nc, "s_lh3")]

    A = nc.alloc_sbuf_tensor
    stats_b = A("stats_b", [128, NCHK], F32)
    stats_r = A("stats_r", [128, NCHK], F32)
    runmax = A("runmax", [128, 1], F32)
    runmaxh = A("runmaxh", [128, 1], F32)
    tmax = A("tmax", [128, 1], F32)
    identbf = A("identbf", [128, 128], BF16)
    identfp = A("identfp", [128, 128], F32)
    maxpad = A("maxpad", [128, 128], F32)
    onesf = A("onesf", [1, 128], F32)
    mrow = A("mrow", [1, 128], F32)
    gmax1 = A("gmax1", [1, 1], F32)
    gmax2 = A("gmax2", [1, 1], F32)
    cxt = A("cxt", [1, 1], F32)
    cdt = A("cdt", [1, 1], F32)
    sgt = A("sgt", [1, 4], F32)
    sxb = A("sxb", [128, 1], F32)
    isxb = A("isxb", [128, 1], F32)
    shb = A("shb", [128, 1], F32)
    ishb = A("ishb", [128, 1], F32)
    z128 = A("z128", [1, 128], F32)
    m32b = A("m32b", [128, 1], F32)
    tiny = A("tinyb", [1, 4], F32)
    wtmp = A("wtmp", [128, 16], F32)

    def vpad(n=1):
        for _ in range(n):
            bd.cp("ve", tiny[:, 0:1], tiny[:, 1:2])

    marks = {}

    blk_cm = nc.Block()
    blk = blk_cm.__enter__()

    with ExitStack() as SC1:
        def SA(name, shape, dt=F32):
            return SC1.enter_context(nc.sbuf_tensor(name, shape, dt))

        def SP(name, shape, dt=F32):
            return SC1.enter_context(nc.psum_tensor(name, shape, dt))

        iotaA = SA("iotaA", [128, 128])
        td_t = SA("td_t", [128, NCB])
        tf_t = SA("tf_t", [128, NCB])
        tmk_t = SA("tmk_t", [128, NCB])
        tmv_t = SA("tmv_t", [128, NCB])
        tmr_t = SA("tmr_t", [128, NCB])
        alpha_t = SA("alpha_t", [128, NCB])
        eu_t = SA("eu_t", [128, NCB])
        e1_t = SA("e1_t", [128, NCB])
        e2_t = SA("e2_t", [128, NCB])
        ab = SA("ab", [128, NCB * 128])
        apow = SA("apow", [128, NCB * 128])
        a128 = SA("a128", [128, NCB * NSC])
        zer128 = SA("zer128", [128, 128])
        wkT = SA("wkT", [128, NCB * 1024], BF16)
        wvT = SA("wvT", [128, NCB * 1024], BF16)
        wrT = SA("wrT", [128, NCB * 1024], BF16)
        woT = SA("woT", [128, NCB * 1024], BF16)

        for vec_d, tile in ((td_d, td_t), (tf_d, tf_t), (tmk_d, tmk_t),
                            (tmv_d, tmv_t), (tmr_d, tmr_t)):
            v3 = vec_d.rearrange("(b p one) -> b p one", p=128, one=1)
            for b_ in range(NCB):
                bd.dma("sy", tile[:, b_:b_ + 1], v3[b_], s_ld)
        bd.dma("sy", cxt[0:1, :], cx_d.rearrange("(a b) -> a b", a=1), s_ld)
        bd.dma("sy", cdt[0:1, :], cd_d.rearrange("(a b) -> a b", a=1), s_ld)
        ld_vecs = s_ld.v

        bd.op("gp", lambda e: e.iota(iotaA[:], pattern=[[1, 128]], base=0,
                                     channel_multiplier=-1,
                                     allow_small_or_imprecise_dtypes=True), s_gp)
        bd.wait("ve", s_gp, s_gp.v)
        bd.ts1("ve", identfp[:], iotaA[:], 0.0, ALU.is_equal)
        bd.cp("ve", identbf[:], identfp[:])
        bd.ms("ve", maxpad[:], 0.0)
        bd.ms("ve", onesf[:], 1.0)
        bd.ms("ve", z128[:], 0.0)
        bd.ms("ve", zer128[:], 0.0)
        bd.ms("ve", tiny[:], 0.0)
        bd.ms("ve", m32b[:], M32)
        bd.ms("ve", sgt[:], 0.0)
        bd.ms("ve", runmax[:], 0.0)
        bd.ms("ve", runmaxh[:], 0.0, sem=s_ve)
        ve_const = s_ve.v
        bd.wait("sy", s_ve, ve_const)
        bd.dma("sy", cc1i.ap(), z128[:], s_ld)
        bd.dma("sy", cc2i.ap(), z128[:], s_ld)

        bd.wait("sc", s_ld, ld_vecs)
        bd.act("sc", e1_t[:], td_t[:], AF.Exp)
        bd.act("sc", eu_t[:], tf_t[:], AF.Exp, sem=s_sc)
        bd.wait("ve", s_sc, s_sc.v)
        bd.ts1("ve", e2_t[:], e1_t[:], -1.0, ALU.mult, sem=s_ve)
        bd.wait("sc", s_ve, s_ve.v)
        bd.act("sc", alpha_t[:], e2_t[:], AF.Exp, sem=s_sc)
        bd.wait("ve", s_sc, s_sc.v)
        for j in range(NCB):
            sl = slice(j * 128, (j + 1) * 128)
            bd.ms("ve", ab[:, sl], 1.0)
            bd.ts1("ve", ab[:, sl], ab[:, sl], alpha_t[:, j:j + 1], ALU.mult)
        for j in range(NCB):
            bd.scan("ve", apow[:, j * 128:j * 128 + 64], ab[:, j * 128:j * 128 + 64],
                    zer128[:, 0:64], 1.0)
        for j in range(NCB):
            bd.scan("ve", apow[:, j * 128 + 64:(j + 1) * 128],
                    ab[:, j * 128:j * 128 + 64], zer128[:, 0:64],
                    apow[:, j * 128 + 63:j * 128 + 64])
        for j in range(NCB):
            s4 = slice(j * NSC, (j + 1) * NSC)
            bd.ms("ve", a128[:, s4], 1.0)
            bd.ts1("ve", a128[:, s4], a128[:, s4],
                   apow[:, j * 128 + CW - 1:j * 128 + CW], ALU.mult)

        with nc.sbuf_tensor("wstg", [128, 2 * 1024], BF16) as wstg, \
             nc.psum_tensor("ps_w", [128, 2048], BF16) as ps_w:
            bd.wait("te", s_ve, ve_const)
            tpms, evms = [], []
            g_i = [0]

            def wprep(w_src, dstT):
                for rb in range(NCB):
                    g = g_i[0]
                    buf = wstg[:, (g % 2) * 1024:((g % 2) + 1) * 1024]
                    pss = ps_w[:, (g % 2) * 1024:((g % 2) + 1) * 1024]
                    if g >= 2:
                        bd.wait("gp", s_te, tpms[g - 2])
                    bd.dma("gp", buf, w_src[rb * 128:(rb + 1) * 128, :], s_st)
                    bd.wait("te", s_st, s_st.v)
                    if g >= 2:
                        bd.wait("te", s_sc, evms[g - 2])
                    for cb in range(NCB):
                        bd.tp("te", pss[:, cb * 128:(cb + 1) * 128],
                              buf[:, cb * 128:(cb + 1) * 128], identbf[:],
                              sem=s_te if cb == NCB - 1 else None)
                    tpms.append(s_te.v)
                    bd.wait("sc", s_te, s_te.v)
                    for cb in range(NCB):
                        bd.cp("sc", dstT[:, cb * 1024 + rb * 128:cb * 1024 + (rb + 1) * 128],
                              pss[:, cb * 128:(cb + 1) * 128],
                              sem=s_sc if cb == NCB - 1 else None)
                    evms.append(s_sc.v)
                    g_i[0] += 1

            wprep(Wk_d, wkT)
            wprep(Wv_d, wvT)
            wprep(Wr_d, wrT)
            wprep(Wo_d, woT)
            marks["wprep_te"] = s_te.v
            bd.flush(blk)

        x_seg = SA("x_seg", [128, 2 * NTC * 1024])
        xn_sb = SA("xn_sb", [128, 1024], BF16)
        xn_fp = SA("xn_fp", [128, NTC * 1024])
        bnraw = SA("bnraw", [128, 24])
        mvs = SA("mvs", [128, 8])
        var4 = SA("var4", [128, 4])
        negm4 = SA("negm4", [128, 4])
        nr = SA("nr", [128, 4])
        ntp = SA("ntp", [128, 4])
        xnT = [SA(f"xnT{j}", [128, SEG + 1], BF16) for j in range(NCB)]
        dmix = SA("dmix", [128, SEG], BF16)
        xkT = [SA(f"xkT{j}", [128, SEG], BF16) for j in range(NCB)]
        xvT = [SA(f"xvT{j}", [128, SEG], BF16) for j in range(NCB)]
        xrT = [SA(f"xrT{j}", [128, SEG], BF16) for j in range(NCB)]
        Ek = [SA(f"Ek{p}", [128, SEG]) for p in range(2)]
        vT = [SA(f"vT{p}", [128, SEG]) for p in range(2)]
        er = [SA(f"er{p}", [128, SEG]) for p in range(2)]
        ekv = SA("ekv", [128, SEG])
        Asc = SA("Asc", [128, SEG + 2])
        Bsc = SA("Bsc", [128, SEG + 2])
        endsA = SA("endsA", [128, NSC])
        endsB = SA("endsB", [128, NSC])
        carrA = SA("carrA", [128, NSC])
        carrB = SA("carrB", [128, NSC])
        segcA = SA("segcA", [128, NCB])
        segcB = SA("segcB", [128, NCB])
        numt = [SA("numt0", [128, SEG]), SA("numt1", [128, SEG])]
        dent = SA("dent", [128, SEG])
        rect = SA("rect", [128, 2 * SEG])
        rwkvT = [SA(f"rwkvT{j}", [128, SEG], BF16) for j in range(NCB)]
        po_sb = [SA("po_sb0", [128, 1024]), SA("po_sb1", [128, 1024])]

        ps_tp = SP("ps_tp", [128, 1024], BF16)
        ps_k = SP("ps_k", [128, SEG])
        ps_v = SP("ps_v", [128, SEG])
        ps_r = SP("ps_r", [128, SEG])
        ps_o = SP("ps_o", [128, 1024])

        bd.ms("ve", segcA[:], 0.0)
        bd.ms("ve", segcB[:], 0.0)
        for j in range(NCB):
            bd.ms("ve", xnT[j][:, 0:1], 0.0)
        if RECIP_ON_GP:
            negone = SA("negone", [128, SEG])
            bd.ms("ve", negone[:], -1.0)

        xload = {}
        stats_mk = {}
        xn_mk = {}
        absm_mk = {}
        tpm_mk = {}
        evac_mk = {}
        mix_mk = {}
        kvr_mk = {}
        exps_mk = {}
        exps_k_mk = {}
        exps_v_mk = {}
        ekv_mk = {}
        scanv_mk = {}
        rwkv_mk = {}
        rect_mk = {}
        oproj_mk = {}
        po_mk = {}
        post_mk = {}
        runmax_final = None

        def emit_a5(s):
            bd.wait("te", s_ve, rwkv_mk[s])
            for tc in range(NTC):
                pk = (s, tc - 1) if tc >= 1 else ((s - 1, NTC - 1) if s >= 1 else None)
                if pk is not None:
                    bd.wait("te", s_sc, po_mk[pk])
                for half in range(2):
                    for j in range(NCB):
                        bd.mm("te", ps_o[:, half * 512:(half + 1) * 512],
                              rwkvT[j][:, tc * 128:(tc + 1) * 128],
                              woT[:, j * 1024 + half * 512:j * 1024 + (half + 1) * 512],
                              j == 0, j == NCB - 1,
                              sem=s_te if (half == 1 and j == NCB - 1) else None)
                oproj_mk[s] = s_te.v
                bd.wait("sc", s_te, oproj_mk[s])
                pk2 = (s, tc - 2) if tc >= 2 else ((s - 1, tc + 2) if s >= 1 else None)
                if pk2 is not None:
                    bd.wait("sc", s_st, post_mk[pk2])
                bd.cp("sc", po_sb[tc % 2][:], ps_o[:], sem=s_sc)
                po_mk[(s, tc)] = s_sc.v
                gc = s * NTC + tc
                bd.wait("gp", s_sc, po_mk[(s, tc)])
                bd.dma("gp", part_d[gc * 128:(gc + 1) * 128, :], po_sb[tc % 2][:], s_st)
                post_mk[(s, tc)] = s_st.v


        def xs_ap(s, m):
            return x_seg[:, ((s % 2) * NTC + m) * 1024:((s % 2) * NTC + m + 1) * 1024]

        def emit_loads(s):
            for m in range(NTC):
                gc = s * NTC + m
                if s == 0 and m == 0:
                    bd.wait("sy", s_te, marks["wprep_te"])
                if s >= 2 and m == 0:
                    bd.wait("sy", s_sc, xn_mk[(s - 1) * NTC - 1])
                bd.dma("sy", xs_ap(s, m), x_d[gc * 128:(gc + 1) * 128, :], s_lx)
                xload[gc] = s_lx.v

        def emit_stats(s):
            for mp in range(NTC // 2):
                m0, m1 = 2 * mp, 2 * mp + 1
                xs0 = xs_ap(s, m0)
                xs1 = xs_ap(s, m1)
                bd.wait("ve", s_lx, xload[s * NTC + NTC - 1])
                bd.bn("ve", bnraw[:, 0:6], xs0[:, 0:512])
                bd.bn("ve", bnraw[:, 6:12], xs0[:, 512:1024])
                bd.bn("ve", bnraw[:, 12:18], xs1[:, 0:512])
                bd.bn("ve", bnraw[:, 18:24], xs1[:, 512:1024])
                bd.bnagg("ve", mvs[:, 2 * m0:2 * m0 + 2], bnraw[:, 0:12])
                bd.bnagg("ve", mvs[:, 2 * m1:2 * m1 + 2], bnraw[:, 12:24])
            vpad(1)
            ga = var4[:, 0:2]
            gb_ = var4[:, 2:4]
            bd.cp("ve", ga[:, 0:1], mvs[:, 1:2])
            bd.cp("ve", ga[:, 1:2], mvs[:, 3:4])
            bd.cp("ve", gb_[:, 0:1], mvs[:, 5:6])
            bd.cp("ve", gb_[:, 1:2], mvs[:, 7:8])
            bd.ts1("ve", ga, ga, EPS, ALU.add)
            bd.ts1("ve", gb_, gb_, EPS, ALU.add)
            bd.ts1("ve", negm4[:, 0:1], mvs[:, 0:1], -1.0, ALU.mult)
            bd.ts1("ve", negm4[:, 1:2], mvs[:, 2:3], -1.0, ALU.mult)
            bd.ts1("ve", negm4[:, 2:3], mvs[:, 4:5], -1.0, ALU.mult)
            bd.ts1("ve", negm4[:, 3:4], mvs[:, 6:7], -1.0, ALU.mult)
            bd.ts2("ve", nr[:, 0:2], ga, -0.5, 1.5, ALU.mult, ALU.add)
            bd.ts2("ve", nr[:, 2:4], gb_, -0.5, 1.5, ALU.mult, ALU.add)
            for it in range(4):
                bd.tt("ve", ntp[:, 0:2], nr[:, 0:2], nr[:, 0:2], ALU.mult)
                bd.tt("ve", ntp[:, 2:4], nr[:, 2:4], nr[:, 2:4], ALU.mult)
                bd.tt("ve", ntp[:, 0:2], ntp[:, 0:2], ga, ALU.mult)
                bd.tt("ve", ntp[:, 2:4], ntp[:, 2:4], gb_, ALU.mult)
                bd.ts2("ve", ntp[:, 0:2], ntp[:, 0:2], -0.5, 1.5, ALU.mult, ALU.add)
                bd.ts2("ve", ntp[:, 2:4], ntp[:, 2:4], -0.5, 1.5, ALU.mult, ALU.add)
                bd.tt("ve", nr[:, 0:2], nr[:, 0:2], ntp[:, 0:2], ALU.mult)
                bd.tt("ve", nr[:, 2:4], nr[:, 2:4], ntp[:, 2:4], ALU.mult)
            vpad(1)
            bd.cp("ve", stats_r[:, s * NTC:(s + 1) * NTC], nr[:, 0:4])
            vpad(1)
            bd.tt("ve", stats_b[:, s * NTC:(s + 1) * NTC], negm4[:], nr[:, 0:4],
                  ALU.mult, sem=s_ve)
            stats_mk[s] = s_ve.v

        def emit_a1b_chunk(s, m):
            gc = s * NTC + m
            if m == 0:
                bd.wait("sc", s_ve, stats_mk[s])
                if s >= 1:
                    bd.wait("sc", s_ve, absm_mk[s - 1])
                    bd.wait("sc", s_ve, mix_mk[s - 1])
            if gc >= 1:
                bd.wait("sc", s_te, tpm_mk[gc - 1])
            xs = xs_ap(s, m)
            bd.act("sc", xn_fp[:, m * 1024:(m + 1) * 1024], xs, AF.Identity,
                   bias=stats_b[:, gc:gc + 1], scale=stats_r[:, gc:gc + 1])
            bd.act("sc", xn_sb[:], xs, AF.Identity,
                   bias=stats_b[:, gc:gc + 1], scale=stats_r[:, gc:gc + 1], sem=s_sc)
            xn_mk[gc] = s_sc.v
            bd.wait("te", s_sc, xn_mk[gc])
            if gc >= 1:
                bd.wait("te", s_sc, evac_mk[gc - 1])
            for j in range(NCB):
                bd.tp("te", ps_tp[:, j * 128:(j + 1) * 128],
                      xn_sb[:, j * 128:(j + 1) * 128], identbf[:],
                      sem=s_te if j == NCB - 1 else None)
            tpm_mk[gc] = s_te.v
            bd.wait("sc", s_te, tpm_mk[gc])
            for j in range(NCB):
                bd.cp("sc", xnT[j][:, 1 + m * 128:1 + (m + 1) * 128],
                      ps_tp[:, j * 128:(j + 1) * 128],
                      sem=s_sc if j == NCB - 1 else None)
            evac_mk[gc] = s_sc.v

        def emit_a1b_reds(s):
            nonlocal runmax_final
            bd.wait("ve", s_sc, xn_mk[s * NTC + NTC - 1])
            bd.red("ve", tmax[:], xn_fp[:], ALU.max, absval=True)
            vpad(1)
            bd.tt("ve", runmax[:], runmax[:], tmax[:], ALU.max, sem=s_ve)
            absm_mk[s] = s_ve.v
            if s == NSEG - 1:
                runmax_final = absm_mk[s]

        def emit_mixes(s):
            bd.wait("ve", s_sc, evac_mk[s * NTC + NTC - 1])
            if s >= 1:
                bd.wait("ve", s_te, kvr_mk[(s - 1, NCB - 1)])
            for j in range(NCB):
                cur = xnT[j][:, 1:SEG + 1]
                prev = xnT[j][:, 0:SEG]
                bd.tt("ve", dmix[:], cur, prev, ALU.subtract)
                bd.stt("ve", xkT[j][:], dmix[:], tmk_t[:, j:j + 1], prev, ALU.mult, ALU.add)
                bd.stt("ve", xvT[j][:], dmix[:], tmv_t[:, j:j + 1], prev, ALU.mult, ALU.add)
                bd.stt("ve", xrT[j][:], dmix[:], tmr_t[:, j:j + 1], prev, ALU.mult, ALU.add)
                bd.cp("ve", xnT[j][:, 0:1], xnT[j][:, SEG:SEG + 1],
                      sem=s_ve if j == NCB - 1 else None)
            mix_mk[s] = s_ve.v

        emit_loads(0)
        emit_stats(0)
        if NSEG > 1:
            emit_loads(1)
        for m in range(NTC):
            emit_a1b_chunk(0, m)
        for s in range(NSEG):
            emit_a1b_reds(s)
            emit_mixes(s)
            if s >= 1:
                emit_a5(s - 1)
            if s + 1 < NSEG:
                emit_stats(s + 1)

            for j in range(NCB):
                pj = j % 2
                bd.wait("te", s_ve, mix_mk[s])
                pkey = (s, j - 1) if j >= 1 else ((s - 1, NCB - 1) if s >= 1 else None)
                if pkey is not None:
                    bd.wait("te", s_sc, exps_k_mk[pkey])
                for cb in range(NCB):
                    bd.mm("te", ps_k[:], wkT[:, cb * 1024 + j * 128:cb * 1024 + (j + 1) * 128],
                          xkT[cb][:], cb == 0, cb == NCB - 1)
                if pkey is not None:
                    bd.wait("te", s_sc, exps_v_mk[pkey])
                for cb in range(NCB):
                    bd.mm("te", ps_v[:], wvT[:, cb * 1024 + j * 128:cb * 1024 + (j + 1) * 128],
                          xvT[cb][:], cb == 0, cb == NCB - 1)
                if pkey is not None:
                    bd.wait("te", s_sc, exps_mk[pkey])
                for cb in range(NCB):
                    bd.mm("te", ps_r[:], wrT[:, cb * 1024 + j * 128:cb * 1024 + (j + 1) * 128],
                          xrT[cb][:], cb == 0, cb == NCB - 1,
                          sem=s_te if cb == NCB - 1 else None)
                kvr_mk[(s, j)] = s_te.v

                bd.wait("sc", s_te, kvr_mk[(s, j)])
                gidx = s * NCB + j
                if gidx >= 2:
                    pk2 = (s, j - 2) if j >= 2 else (s - 1, NCB - 2 + j)
                    bd.wait("sc", s_ve, scanv_mk[pk2])
                bd.act("sc", Ek[pj][:], ps_k[:], AF.Exp, sem=s_sc)
                exps_k_mk[(s, j)] = s_sc.v
                bd.act("sc", vT[pj][:], ps_v[:], AF.Identity, sem=s_sc)
                exps_v_mk[(s, j)] = s_sc.v
                bd.act("sc", er[pj][:], ps_r[:], AF.Exp, scale=-1.0, sem=s_sc)
                exps_mk[(s, j)] = s_sc.v

                bd.wait("ve", s_sc, exps_mk[(s, j)])
                bd.tt("ve", ekv[:], Ek[pj][:], vT[pj][:], ALU.mult)
                bd.cp("ve", Asc[:, 1:2], segcA[:, j:j + 1])
                bd.cp("ve", Bsc[:, 1:2], segcB[:, j:j + 1])
                abj = ab[:, j * 128:j * 128 + CW]
                for ch in range(NSC):
                    bd.scan("ve", Asc[:, 2 + ch * CW:2 + (ch + 1) * CW], abj,
                            ekv[:, ch * CW:(ch + 1) * CW], 0.0)
                    bd.scan("ve", Bsc[:, 2 + ch * CW:2 + (ch + 1) * CW], abj,
                            Ek[pj][:, ch * CW:(ch + 1) * CW], 0.0)
                for ch in range(NSC):
                    bd.cp("ve", endsA[:, ch:ch + 1],
                          Asc[:, 1 + (ch + 1) * CW:2 + (ch + 1) * CW])
                    bd.cp("ve", endsB[:, ch:ch + 1],
                          Bsc[:, 1 + (ch + 1) * CW:2 + (ch + 1) * CW])
                a128j = a128[:, j * NSC:(j + 1) * NSC]
                bd.scan("ve", carrA[:], a128j, endsA[:], segcA[:, j:j + 1])
                bd.scan("ve", carrB[:], a128j, endsB[:], segcB[:, j:j + 1])
                apj = apow[:, j * 128:j * 128 + CW]
                for ch in range(NSC):
                    scalA = segcA[:, j:j + 1] if ch == 0 else carrA[:, ch - 1:ch]
                    scalB = segcB[:, j:j + 1] if ch == 0 else carrB[:, ch - 1:ch]
                    sl = slice(2 + ch * CW, 2 + (ch + 1) * CW)
                    bd.stt("ve", Asc[:, sl], apj, scalA, Asc[:, sl], ALU.mult, ALU.add)
                    bd.stt("ve", Bsc[:, sl], apj, scalB, Bsc[:, sl], ALU.mult, ALU.add)
                bd.cp("ve", segcA[:, j:j + 1], carrA[:, NSC - 1:NSC])
                bd.cp("ve", segcB[:, j:j + 1], carrB[:, NSC - 1:NSC])
                bd.stt("ve", numt[pj][:], ekv[:], eu_t[:, j:j + 1], Asc[:, 1:SEG + 1],
                       ALU.mult, ALU.add)
                if j >= 1:
                    bd.wait("ve", s_sc, rect_mk[(s, j - 1)])
                    if j == 1 and s >= 1:
                        bd.wait("ve", s_te, oproj_mk[s - 1])
                    jm = j - 1
                    bd.tt("ve", rwkvT[jm][:], numt[1 - pj][:],
                          rect[:, (jm % 2) * SEG:(jm % 2 + 1) * SEG], ALU.mult)
                bd.stt("ve", dent[:], Ek[pj][:], eu_t[:, j:j + 1], Bsc[:, 1:SEG + 1],
                       ALU.mult, ALU.add)
                bd.stt("ve", dent[:], er[pj][:], 1.0, dent[:], ALU.add, ALU.mult, sem=s_ve)
                scanv_mk[(s, j)] = s_ve.v
                if 2 <= j <= 5 and s + 1 < NSEG:
                    emit_a1b_chunk(s + 1, j - 2)
                bd.wait("sc", s_ve, scanv_mk[(s, j)])
                rsl = rect[:, (j % 2) * SEG:(j % 2 + 1) * SEG]
                bd.act("sc", rsl, dent[:], AF.Ln)
                bd.act("sc", rsl, rsl, AF.Exp, scale=-1.0, sem=s_sc)
                rect_mk[(s, j)] = s_sc.v
                if j == NCB - 1:
                    bd.wait("ve", s_sc, rect_mk[(s, j)])
                    bd.tt("ve", rwkvT[j][:], numt[pj][:], rsl, ALU.mult, sem=s_ve)
                    if s + 2 < NSEG:
                        emit_loads(s + 2)
            rwkv_mk[s] = s_ve.v

        emit_a5(NSEG - 1)
        marks["attn_ve"] = s_ve.v
        marks["attn_sc"] = s_sc.v
        marks["attn_te"] = s_te.v
        marks["part_st"] = s_st.v

        bd.wait("ve", s_ve, runmax_final)
        bd.cp("ve", maxpad[:, 0:1], runmax[:], sem=s_ve)
        mp_ve = s_ve.v
        with nc.psum_tensor("ps_bc", [128, 128]) as ps_bc:
            bd.wait("te", s_ve, mp_ve)
            bd.tp("te", ps_bc[:], maxpad[:], identfp[:], sem=s_te)
            tp1_te = s_te.v
            bd.wait("ve", s_te, tp1_te)
            bd.red("ve", gmax1[0:1, :], ps_bc[0:1, :], ALU.max, sem=s_ve)
            bd.wait("sy", s_ve, s_ve.v)
            bd.dma("sy", cc1i.ap()[0:1, 0:1], gmax1[0:1, :], s_ld)
            bd.wait("gp", s_ld, s_ld.v)
            bd.op("gp", lambda e: e.collective_compute(
                "AllReduce", ALU.max, replica_groups=[[0, 1, 2, 3, 4, 5, 6, 7]],
                ins=[cc1i.ap().opt()], outs=[cc1o.ap().opt()]), s_c1)
            bd.wait("sy", s_c1, 1)
            bd.dma("sy", gmax1[0:1, :], cc1o.ap()[0:1, 0:1], s_ld)
            gmax1_ld = s_ld.v
            bd.wait("sc", s_ld, gmax1_ld)
            bd.act("sc", sgt[0:1, 0:1], cxt[0:1, :], AF.Exp, scale=-1.0, sem=s_sc)
            bd.wait("ve", s_sc, s_sc.v)
            bd.ts1("ve", sgt[0:1, 1:2], sgt[0:1, 0:1], 1.0, ALU.add)
            vpad(1)
            bd.recip("ve", sgt[0:1, 1:2], sgt[0:1, 1:2])
            vpad(1)
            bd.tt("ve", sgt[0:1, 2:3], sgt[0:1, 1:2], gmax1[0:1, :], ALU.mult)
            vpad(1)
            bd.ts1("ve", sgt[0:1, 3:4], sgt[0:1, 2:3], 1.0 / 127.0, ALU.mult)
            vpad(1)
            bd.recip("ve", sgt[0:1, 0:1], sgt[0:1, 3:4], sem=s_ve)
            sx_ve = s_ve.v
            bd.wait("te", s_ve, sx_ve)
            bd.mm("te", ps_bc[:, 0:1], onesf[0:1, :], sgt[0:1, 3:4], True, True)
            bd.mm("te", ps_bc[:, 1:2], onesf[0:1, :], sgt[0:1, 0:1], True, True, sem=s_te)
            bd.wait("ve", s_te, s_te.v)
            bd.cp("ve", sxb[:], ps_bc[:, 0:1])
            bd.cp("ve", isxb[:], ps_bc[:, 1:2])
            vpad(2)
            bd.cp("ve", tiny[:, 2:3], tiny[:, 3:4], sem=s_ve)
            marks["bc1_ve"] = s_ve.v
            bd.flush(blk)

    with ExitStack() as SC2:
        def SA2(name, shape, dt=F32):
            return SC2.enter_context(nc.sbuf_tensor(name, shape, dt))

        wgT = SA2("wgT", [128, NCB * I], BF16)
        wuT = SA2("wuT", [128, NCB * I], BF16)
        wg_rs = SA2("wg_rs", [128, NIB])
        wu_rs = SA2("wu_rs", [128, NIB])
        gsr = SA2("gsr", [128, NIB])
        hgsr = SA2("hgsr", [128, NIB])
        usr = SA2("usr", [128, NIB])
        wfst = [SA2(f"wfst{p}", [128, C]) for p in range(4)]
        qst = [SA2(f"qst{p}", [128, C], BF16) for p in range(4)]
        xf = [SA2(f"xf{p}", [128, 1024]) for p in range(2)]
        rs_isx = SA2("rs_isx", [128, NCHK])
        bs_isx = SA2("bs_isx", [128, NCHK])
        qxtm = SA2("qxtm", [128, 1024])
        qxbf = SA2("qxbf", [128, 1024], BF16)
        qxT = [SA2(f"qxT{cb}", [128, 512], BF16) for cb in range(NCB)]
        thh = SA2("thh", [128, 1024])
        ggg = SA2("ggg", [128, 1024])
        uuu = SA2("uuu", [128, 1024])
        guu = SA2("guu", [128, 512])
        hsb = [SA2(f"hsb{p}", [128, 512]) for p in range(2)]
        hmax2 = SA2("hmax2", [128, 2])

        bd.wait("sy", s_ve, marks["attn_ve"])
        bd.wait("sy", s_sc, marks["attn_sc"])
        bd.wait("sy", s_te, marks["attn_te"])

        wq_tp_mk = []
        wq_ev_mk = []
        wq_q_mk = []
        with nc.psum_tensor("ps_wq", [128, 4096], BF16) as ps_wq:
            for ib in range(NIB):
                par2 = ib % 2
                if ib >= 2:
                    bd.wait("sy", s_ve, wq_q_mk[ib - 2])
                bd.dma("sy", wfst[2 * par2][:], wg_d[ib * 128:(ib + 1) * 128, :], s_ld)
                bd.dma("sy", wfst[2 * par2 + 1][:], wu_d[ib * 128:(ib + 1) * 128, :], s_ld)
                wld = s_ld.v
                bd.wait("ve", s_ld, wld)
                if ib >= 2:
                    bd.wait("ve", s_te, wq_tp_mk[ib - 2])
                w0 = wfst[2 * par2]
                w1 = wfst[2 * par2 + 1]
                q0 = qst[2 * par2]
                q1 = qst[2 * par2 + 1]
                c4 = 8 * par2
                bd.red("ve", wtmp[:, c4 + 0:c4 + 1], w0[:], ALU.max, absval=True)
                bd.red("ve", wtmp[:, c4 + 1:c4 + 2], w1[:], ALU.max, absval=True)
                bd.recip("ve", wtmp[:, c4 + 2:c4 + 3], wtmp[:, c4 + 0:c4 + 1])
                bd.recip("ve", wtmp[:, c4 + 3:c4 + 4], wtmp[:, c4 + 1:c4 + 2])
                bd.ts1("ve", wtmp[:, c4 + 4:c4 + 5], wtmp[:, c4 + 2:c4 + 3], 127.0, ALU.mult)
                bd.ts1("ve", wtmp[:, c4 + 5:c4 + 6], wtmp[:, c4 + 3:c4 + 4], 127.0, ALU.mult)
                bd.ts1("ve", wg_rs[:, ib:ib + 1], wtmp[:, c4 + 0:c4 + 1], 1.0 / 127.0, ALU.mult)
                bd.ts1("ve", wu_rs[:, ib:ib + 1], wtmp[:, c4 + 1:c4 + 2], 1.0 / 127.0, ALU.mult)
                for k, (ww, qq) in enumerate(((w0, q0), (w1, q1))):
                    bd.ts2("ve", ww[:], ww[:], wtmp[:, c4 + 4 + k:c4 + 5 + k], M32,
                           ALU.mult, ALU.add)
                    bd.ts2("ve", ww[:], ww[:], M32 + 127.0, M32 - 128.0,
                           ALU.min, ALU.max)
                    bd.ts1("ve", qq[:], ww[:], M32, ALU.subtract,
                           sem=s_ve if k == 1 else None)
                wq_q_mk.append(s_ve.v)
                bd.wait("te", s_ve, wq_q_mk[-1])
                if ib >= 2:
                    bd.wait("te", s_sc, wq_ev_mk[ib - 2])
                for k, qq in ((0, q0), (1, q1)):
                    for cb in range(NCB):
                        bd.tp("te", ps_wq[:, par2 * 2048 + k * 1024 + cb * 128:
                                          par2 * 2048 + k * 1024 + (cb + 1) * 128],
                              qq[:, cb * 128:(cb + 1) * 128], identbf[:],
                              sem=s_te if (k == 1 and cb == NCB - 1) else None)
                wq_tp_mk.append(s_te.v)
                bd.wait("sc", s_te, wq_tp_mk[-1])
                for k, dstT in ((0, wgT), (1, wuT)):
                    for cb in range(NCB):
                        bd.cp("sc", dstT[:, cb * I + ib * 128:cb * I + (ib + 1) * 128],
                              ps_wq[:, par2 * 2048 + k * 1024 + cb * 128:
                                       par2 * 2048 + k * 1024 + (cb + 1) * 128],
                              sem=s_sc if (k == 1 and cb == NCB - 1) else None)
                wq_ev_mk.append(s_sc.v)
            bd.flush(blk)

        ps_qx = SC2.enter_context(nc.psum_tensor("ps_qx", [128, 1024], BF16))
        ps_g = SC2.enter_context(nc.psum_tensor("ps_g", [128, 1024]))
        ps_u = SC2.enter_context(nc.psum_tensor("ps_u", [128, 1024]))
        bd.wait("ve", s_ve, marks["bc1_ve"])
        bd.ts1("ve", gsr[:], wg_rs[:], sxb[:, 0:1], ALU.mult)
        bd.ts1("ve", usr[:], wu_rs[:], sxb[:, 0:1], ALU.mult)
        bd.ts1("ve", hgsr[:], gsr[:], 0.5, ALU.mult)
        bd.ts1("ve", usr[:], usr[:], 0.5, ALU.mult)
        bd.ts1("ve", rs_isx[:], stats_r[:], isxb[:, 0:1], ALU.mult)
        bd.ts1("ve", bs_isx[:], stats_b[:], isxb[:, 0:1], ALU.mult, sem=s_ve)
        scl_ve = s_ve.v

        hstore_mk = {}
        f1q_mk = {}
        f1tp_mk = {}
        f1ev_mk = {}
        f1gu_mk = {}
        f1sc_mk = {}
        f1h_mk = {}
        for grp in range(NGRP):
            for m4 in range(4):
                gc = grp * 4 + m4
                p = gc % 2
                if gc >= 2:
                    bd.wait("sy", s_ve, f1q_mk[gc - 2])
                bd.dma("sy", xf[p][:], x_d[gc * 128:(gc + 1) * 128, :], s_lf[p])
                bd.wait("sc", s_lf[p], s_lf[p].v)
                if gc == 0:
                    bd.wait("sc", s_ve, scl_ve)
                if gc >= 1:
                    bd.wait("sc", s_ve, f1q_mk[gc - 1])
                bd.act("sc", qxtm[:], xf[p][:], AF.Identity,
                       bias=bs_isx[:, gc:gc + 1], scale=rs_isx[:, gc:gc + 1], sem=s_sc)
                xnf_mk = s_sc.v
                bd.wait("ve", s_sc, xnf_mk)
                if gc >= 1:
                    bd.wait("ve", s_te, f1tp_mk[gc - 1])
                bd.ts2("ve", qxtm[:], qxtm[:], M32, M32 + 127.0, ALU.add, ALU.min)
                bd.ts2("ve", qxbf[:], qxtm[:], M32 - 128.0, M32, ALU.max, ALU.subtract,
                       sem=s_ve)
                f1q_mk[gc] = s_ve.v
                bd.wait("te", s_ve, f1q_mk[gc])
                if gc >= 1:
                    bd.wait("te", s_sc, f1ev_mk[gc - 1])
                for cb in range(NCB):
                    bd.tp("te", ps_qx[:, cb * 128:(cb + 1) * 128],
                          qxbf[:, cb * 128:(cb + 1) * 128], identbf[:],
                          sem=s_te if cb == NCB - 1 else None)
                f1tp_mk[gc] = s_te.v
                bd.wait("sc", s_te, f1tp_mk[gc])
                if m4 == 0 and grp >= 1:
                    bd.wait("sc", s_te, f1gu_mk[(grp - 1, NIB - 1)])
                for cb in range(NCB):
                    bd.cp("sc", qxT[cb][:, m4 * 128:(m4 + 1) * 128],
                          ps_qx[:, cb * 128:(cb + 1) * 128],
                          sem=s_sc if cb == NCB - 1 else None)
                f1ev_mk[gc] = s_sc.v
            for ib in range(NIB):
                p = ib % 2
                gi = grp * NIB + ib
                psg = ps_g[:, p * 512:(p + 1) * 512]
                psu = ps_u[:, p * 512:(p + 1) * 512]
                bd.wait("te", s_sc, f1ev_mk[grp * 4 + 3])
                if gi >= 2:
                    pk = (grp, ib - 2) if ib >= 2 else (grp - 1, NIB - 2 + ib)
                    bd.wait("te", s_sc, f1sc_mk[pk])
                for cb in range(NCB):
                    bd.mm("te", psg, wgT[:, cb * I + ib * 128:cb * I + (ib + 1) * 128],
                          qxT[cb][:], cb == 0, cb == NCB - 1)
                for cb in range(NCB):
                    bd.mm("te", psu, wuT[:, cb * I + ib * 128:cb * I + (ib + 1) * 128],
                          qxT[cb][:], cb == 0, cb == NCB - 1,
                          sem=s_te if cb == NCB - 1 else None)
                f1gu_mk[(grp, ib)] = s_te.v
                bd.wait("sc", s_te, f1gu_mk[(grp, ib)])
                if gi >= 2:
                    pk = (grp, ib - 2) if ib >= 2 else (grp - 1, NIB - 2 + ib)
                    bd.wait("sc", s_ve, f1h_mk[pk])
                th2 = thh[:, p * 512:(p + 1) * 512]
                gg2 = ggg[:, p * 512:(p + 1) * 512]
                uu2 = uuu[:, p * 512:(p + 1) * 512]
                bd.act("sc", th2, psg, AF.Tanh, scale=hgsr[:, ib:ib + 1])
                bd.act("sc", gg2, psg, AF.Identity, scale=gsr[:, ib:ib + 1])
                bd.act("sc", uu2, psu, AF.Identity, scale=usr[:, ib:ib + 1], sem=s_sc)
                f1sc_mk[(grp, ib)] = s_sc.v
                bd.wait("ve", s_sc, f1sc_mk[(grp, ib)])
                if gi >= 2:
                    pk = (grp, ib - 2) if ib >= 2 else (grp - 1, NIB - 2 + ib)
                    bd.wait("ve", s_st, hstore_mk[pk])
                bd.tt("ve", guu[:], gg2, uu2, ALU.mult)
                bd.stt("ve", hsb[p][:], th2, 1.0, guu[:], ALU.add, ALU.mult)
                bd.red("ve", hmax2[:, p:p + 1], hsb[p][:], ALU.max, absval=True, sem=s_ve)
                f1h_mk[(grp, ib)] = s_ve.v
                if gi >= 1:
                    pp = 1 - p
                    bd.tt("ve", runmaxh[:], runmaxh[:], hmax2[:, pp:pp + 1], ALU.max)
                bd.wait("gp", s_ve, f1h_mk[(grp, ib)])
                bd.dma("gp", hsp_d[ib * 128:(ib + 1) * 128, grp * 512:(grp + 1) * 512],
                       hsb[p][:], s_st)
                hstore_mk[(grp, ib)] = s_st.v
        lastp = ((NGRP - 1) * NIB + NIB - 1) % 2
        vpad(1)
        bd.tt("ve", runmaxh[:], runmaxh[:], hmax2[:, lastp:lastp + 1], ALU.max, sem=s_ve)
        runmaxh_final = s_ve.v
        marks["f1_st"] = s_st.v
        marks["f1_ve"] = s_ve.v
        marks["f1_sc"] = s_sc.v
        marks["f1_te"] = s_te.v

        bd.wait("ve", s_ve, runmaxh_final)
        bd.cp("ve", maxpad[:, 0:1], runmaxh[:], sem=s_ve)
        mp2_ve = s_ve.v
        with nc.psum_tensor("ps_mx2", [128, 128]) as ps_mx2:
            bd.wait("te", s_ve, mp2_ve)
            bd.wait("te", s_sc, marks["f1_sc"])
            bd.tp("te", ps_mx2[:], maxpad[:], identfp[:], sem=s_te)
            bd.wait("ve", s_te, s_te.v)
            bd.red("ve", gmax2[0:1, :], ps_mx2[0:1, :], ALU.max, sem=s_ve)
            bd.wait("sy", s_ve, s_ve.v)
            bd.dma("sy", cc2i.ap()[0:1, 0:1], gmax2[0:1, :], s_ld)
            bd.wait("gp", s_ld, s_ld.v)
            bd.op("gp", lambda e: e.collective_compute(
                "AllReduce", ALU.max, replica_groups=[[0, 1, 2, 3, 4, 5, 6, 7]],
                ins=[cc2i.ap().opt()], outs=[cc2o.ap().opt()]), s_c2)
            bd.flush(blk)

    with ExitStack() as SC3:
        def SA3(name, shape, dt=F32):
            return SC3.enter_context(nc.sbuf_tensor(name, shape, dt))

        wdT = SA3("wdT", [128, NIB * 1024], BF16)
        wd_rs = SA3("wd_rs", [128, NCB])
        hl = [SA3(f"hl{p}", [128, 512]) for p in range(4)]
        xf2 = [SA3(f"xf2{p}", [128, 1024]) for p in range(2)]
        qh = [SA3(f"qh{p}", [128, 512], BF16) for p in range(4)]
        part_sb = [SA3(f"part_sb{p}", [128, 4096]) for p in range(2)]
        outs_sb = [SA3(f"outs_sb{p}", [128, 1024]) for p in range(2)]
        wdst = [SA3(f"wdst{p}", [128, I]) for p in range(2)]
        qdst = [SA3(f"qdst{p}", [128, I], BF16) for p in range(2)]

        with nc.psum_tensor("ps_wd", [128, 2048], BF16) as ps_wd:
            bd.wait("sy", s_ve, marks["f1_ve"])
            bd.wait("sy", s_sc, marks["f1_sc"])
            bd.wait("sy", s_te, marks["f1_te"])
            wd_q_mk = []
            wd_tp_last = []
            wd_ev_last = []
            for cbp in range(NCB // 2):
                c0, c1 = 2 * cbp, 2 * cbp + 1
                if cbp >= 1:
                    bd.wait("sy", s_ve, wd_q_mk[-1])
                bd.dma("sy", wdst[0][:], wd_d[c0 * 128:(c0 + 1) * 128, :], s_ld)
                bd.dma("sy", wdst[1][:], wd_d[c1 * 128:(c1 + 1) * 128, :], s_ld)
                bd.wait("ve", s_ld, s_ld.v)
                if cbp >= 1:
                    bd.wait("ve", s_te, wd_tp_last[-1])
                bd.red("ve", wtmp[:, 0:1], wdst[0][:], ALU.max, absval=True)
                bd.red("ve", wtmp[:, 1:2], wdst[1][:], ALU.max, absval=True)
                bd.recip("ve", wtmp[:, 2:3], wtmp[:, 0:1])
                bd.recip("ve", wtmp[:, 3:4], wtmp[:, 1:2])
                bd.ts1("ve", wtmp[:, 4:5], wtmp[:, 2:3], 127.0, ALU.mult)
                bd.ts1("ve", wtmp[:, 5:6], wtmp[:, 3:4], 127.0, ALU.mult)
                bd.ts1("ve", wd_rs[:, c0:c0 + 1], wtmp[:, 0:1], 1.0 / 127.0, ALU.mult)
                bd.ts1("ve", wd_rs[:, c1:c1 + 1], wtmp[:, 1:2], 1.0 / 127.0, ALU.mult)
                for k, cc in ((0, c0), (1, c1)):
                    bd.ts2("ve", wdst[k][:], wdst[k][:], wtmp[:, 4 + k:5 + k], M32,
                           ALU.mult, ALU.add)
                    bd.ts2("ve", wdst[k][:], wdst[k][:], M32 + 127.0, M32 - 128.0,
                           ALU.min, ALU.max)
                    bd.ts1("ve", wdst[k][:], wdst[k][:], M32, ALU.subtract)
                    bd.ts1("ve", qdst[k][:], wdst[k][:], wd_rs[:, cc:cc + 1], ALU.mult,
                           sem=s_ve if k == 1 else None)
                wd_q_mk.append(s_ve.v)
                bd.wait("te", s_ve, wd_q_mk[-1])
                for k, cc in ((0, c0), (1, c1)):
                    for ig in range((NIB + 7) // 8):
                        n_in_g = min(8, NIB - ig * 8)
                        if wd_ev_last:
                            bd.wait("te", s_sc, wd_ev_last[-1])
                        for t in range(n_in_g):
                            ib = ig * 8 + t
                            bd.tp("te", ps_wd[:, k * 1024 + t * 128:k * 1024 + (t + 1) * 128],
                                  qdst[k][:, ib * 128:(ib + 1) * 128], identbf[:],
                                  sem=s_te if t == n_in_g - 1 else None)
                        wd_tp_last.append(s_te.v)
                        bd.wait("sc", s_te, wd_tp_last[-1])
                        for t in range(n_in_g):
                            ib = ig * 8 + t
                            bd.cp("sc", wdT[:, ib * 1024 + cc * 128:ib * 1024 + (cc + 1) * 128],
                                  ps_wd[:, k * 1024 + t * 128:k * 1024 + (t + 1) * 128],
                                  sem=s_sc if t == n_in_g - 1 else None)
                        wd_ev_last.append(s_sc.v)
            wd_done_sc = s_sc.v
            bd.flush(blk)

        bd.wait("sy", s_c2, 1)
        bd.dma("sy", gmax2[0:1, :], cc2o.ap()[0:1, 0:1], s_ld)
        gmax2_ld = s_ld.v
        bd.wait("sc", s_ld, gmax2_ld)
        bd.act("sc", sgt[0:1, 0:1], cdt[0:1, :], AF.Exp, scale=-1.0, sem=s_sc)
        bd.wait("ve", s_sc, s_sc.v)
        bd.ts1("ve", sgt[0:1, 1:2], sgt[0:1, 0:1], 1.0, ALU.add)
        vpad(1)
        bd.recip("ve", sgt[0:1, 1:2], sgt[0:1, 1:2])
        vpad(1)
        bd.tt("ve", sgt[0:1, 2:3], sgt[0:1, 1:2], gmax2[0:1, :], ALU.mult)
        vpad(1)
        bd.ts1("ve", sgt[0:1, 3:4], sgt[0:1, 2:3], 1.0 / 127.0, ALU.mult)
        vpad(1)
        bd.recip("ve", sgt[0:1, 0:1], sgt[0:1, 3:4], sem=s_ve)
        sh_ve = s_ve.v
        with nc.psum_tensor("ps_bc2", [128, 2]) as ps_bc2:
            bd.wait("te", s_ve, sh_ve)
            bd.mm("te", ps_bc2[:, 0:1], onesf[0:1, :], sgt[0:1, 3:4], True, True)
            bd.mm("te", ps_bc2[:, 1:2], onesf[0:1, :], sgt[0:1, 0:1], True, True, sem=s_te)
            bd.wait("ve", s_te, s_te.v)
            bd.cp("ve", shb[:], ps_bc2[:, 0:1])
            bd.cp("ve", ishb[:], ps_bc2[:, 1:2])
            vpad(2)
            bd.cp("ve", tiny[:, 2:3], tiny[:, 3:4], sem=s_ve)
            bc2_ve = s_ve.v
            bd.flush(blk)
        wsd_ve = bc2_ve

        ps_d = SC3.enter_context(nc.psum_tensor("ps_d", [128, 4096]))
        f2q_mk = {}
        f2sc_mk = {}
        f2te_mk = {}
        f2ev_mk = {}
        f2st_mk = {}
        bd.wait("sy", s_st, marks["part_st"])
        bd.wait("sy", s_st, marks["f1_st"])
        for grp in range(NGRP):
            for m4 in range(4):
                gc = grp * 4 + m4
                if grp >= 2 and m4 == 0:
                    bd.wait("sy", s_ve, f2ev_mk[(grp - 2, 3)])
                bd.dma("sy", part_sb[grp % 2][:, m4 * 1024:(m4 + 1) * 1024],
                       part_d[gc * 128:(gc + 1) * 128, :], s_lp[grp % 2])
            part_ld = s_lp[grp % 2].v
            for ib in range(NIB):
                p = ib % 4
                gi = grp * NIB + ib
                if gi >= 4:
                    pk4 = (grp, ib - 4) if ib >= 4 else (grp - 1, NIB - 4 + ib)
                    bd.wait("sy", s_ve, f2q_mk[pk4])
                bd.dma("sy", hl[p][:], hsp_d[ib * 128:(ib + 1) * 128,
                                             grp * 512:(grp + 1) * 512], s_lh2[p])
                hld = s_lh2[p].v
                bd.wait("sc", s_lh2[p], hld)
                if gi == 0:
                    bd.wait("sc", s_ve, wsd_ve)
                bd.act("sc", hl[p][:], hl[p][:], AF.Identity,
                       bias=m32b[:, 0:1], scale=ishb[:, 0:1], sem=s_sc)
                f2sc_mk[(grp, ib)] = s_sc.v
                bd.wait("ve", s_sc, f2sc_mk[(grp, ib)])
                if gi >= 2:
                    pk = (grp, ib - 2) if ib >= 2 else (grp - 1, NIB - 2 + ib)
                    bd.wait("ve", s_te, f2te_mk[pk])
                bd.ts2("ve", hl[p][:], hl[p][:], M32 + 127.0, M32 - 128.0, ALU.min, ALU.max)
                bd.ts1("ve", qh[p][:], hl[p][:], M32, ALU.subtract, sem=s_ve)
                f2q_mk[(grp, ib)] = s_ve.v
                if debug and grp == 4 and ib == 0:
                    qh_cap = SC3.enter_context(nc.sbuf_tensor("qh_cap", [128, 512], BF16))
                    bd.cp("ve", qh_cap[:], qh[p][:], sem=s_ve)
                    bd.wait("gp", s_ve, s_ve.v)
                    bd.dma("gp", dbg_qh, qh_cap[:], s_st)
                bd.wait("te", s_ve, f2q_mk[(grp, ib)])
                if gi == 0:
                    bd.wait("te", s_sc, wd_done_sc)
                if ib == 0 and grp >= 1:
                    bd.wait("te", s_ve, f2ev_mk[(grp - 1, 3)])
                for tc in range(4):
                    for half in range(2):
                        bd.mm("te", ps_d[:, tc * 1024 + half * 512:tc * 1024 + (half + 1) * 512],
                              qh[p][:, tc * 128:(tc + 1) * 128],
                              wdT[:, ib * 1024 + half * 512:ib * 1024 + (half + 1) * 512],
                              ib == 0, ib == NIB - 1,
                              sem=s_te if (tc == 3 and half == 1) else None)
                f2te_mk[(grp, ib)] = s_te.v
            bd.wait("ve", s_te, f2te_mk[(grp, NIB - 1)])
            bd.wait("ve", s_lp[grp % 2], part_ld)
            if debug and grp == 4:
                psc = SC3.enter_context(nc.sbuf_tensor("psc", [128, 4096], F32))
                bd.cp("ve", psc[:, 0:2048], ps_d[:, 0:2048])
                bd.cp("ve", psc[:, 2048:4096], ps_d[:, 2048:4096], sem=s_ve)
                bd.wait("gp", s_ve, s_ve.v)
                bd.dma("gp", dbg_ps, psc[:], s_st)
                bd.dma("gp", dbg_part4, part_sb[grp % 2][:], s_st)
            for tc in range(4):
                gc = grp * 4 + tc
                if grp * 4 + tc >= 2:
                    pgc = grp * 4 + tc - 2
                    bd.wait("sy", s_ve, f2ev_mk[(pgc // 4, pgc % 4)])
                bd.dma("sy", xf2[tc % 2][:], x_d[gc * 128:(gc + 1) * 128, :],
                       s_lf[tc % 2])
                xf2_ld = s_lf[tc % 2].v
                pk = (grp, tc - 2) if tc >= 2 else ((grp - 1, tc + 2) if grp >= 1 else None)
                if pk is not None:
                    bd.wait("ve", s_st, f2st_mk[pk])
                bd.wait("ve", s_lf[tc % 2], xf2_ld)
                for half in range(2):
                    sl = slice(tc * 1024 + half * 512, tc * 1024 + (half + 1) * 512)
                    hsl = slice(half * 512, (half + 1) * 512)
                    bd.ts1("ve", outs_sb[tc % 2][:, hsl], ps_d[:, sl], shb[:, 0:1], ALU.mult)
                    bd.tt("ve", outs_sb[tc % 2][:, hsl], outs_sb[tc % 2][:, hsl],
                          part_sb[grp % 2][:, tc * 1024 + half * 512:
                                           tc * 1024 + (half + 1) * 512], ALU.add)
                    bd.tt("ve", outs_sb[tc % 2][:, hsl], outs_sb[tc % 2][:, hsl],
                          xf2[tc % 2][:, hsl], ALU.add,
                          sem=s_ve if half == 1 else None)
                f2ev_mk[(grp, tc)] = s_ve.v
                bd.wait("gp", s_ve, f2ev_mk[(grp, tc)])
                bd.dma("gp", out_d[gc * 128:(gc + 1) * 128, :], outs_sb[tc % 2][:], s_st)
                f2st_mk[(grp, tc)] = s_st.v

        if debug:
            NCHKd = T // 128
            dstg = SC3.enter_context(nc.sbuf_tensor("dstg", [128, 2 * NCHKd + 16], F32))
            bd.wait("ve", s_sc, s_sc.v)
            bd.ms("ve", dstg[:], 0.0)
            bd.cp("ve", dstg[:, 0:NCHKd], stats_b[:])
            bd.cp("ve", dstg[:, NCHKd:2 * NCHKd], stats_r[:])
            bd.cp("ve", dstg[0:1, 2 * NCHKd + 0:2 * NCHKd + 1], gmax1[0:1, :])
            bd.cp("ve", dstg[0:1, 2 * NCHKd + 1:2 * NCHKd + 2], gmax2[0:1, :])
            bd.cp("ve", dstg[:, 2 * NCHKd + 2:2 * NCHKd + 3], sxb[:])
            bd.cp("ve", dstg[:, 2 * NCHKd + 3:2 * NCHKd + 4], isxb[:])
            bd.cp("ve", dstg[:, 2 * NCHKd + 4:2 * NCHKd + 5], shb[:])
            bd.cp("ve", dstg[:, 2 * NCHKd + 5:2 * NCHKd + 6], ishb[:])
            bd.cp("ve", dstg[:, 2 * NCHKd + 6:2 * NCHKd + 7], runmax[:])
            bd.cp("ve", dstg[:, 2 * NCHKd + 7:2 * NCHKd + 8], runmaxh[:], sem=s_ve)
            bd.wait("gp", s_ve, s_ve.v)
            bd.dma("gp", dbg_pers[:], dstg[:], s_st)
            bd.dma("gp", dbg_wdT, wdT[:], s_st)
        bd.wait("gp", s_st, s_st.v)
        bd.flush(blk)

    blk_cm.__exit__(None, None, None)
    return nc


_NC_CACHE = {}


def kernel(L0_attn_x, L0_attn_time_decay, L0_attn_time_first, L0_attn_time_mix_k,
           L0_attn_time_mix_v, L0_attn_time_mix_r, L0_attn_Wk, L0_attn_Wv, L0_attn_Wr,
           L0_attn_Wo, L0_ffn_clip_x, L0_ffn_clip_down, L0_ffn_w_gate, L0_ffn_w_up,
           L0_ffn_w_down):
    x = np.ascontiguousarray(np.asarray(L0_attn_x, dtype=np.float32))
    Bn, T, Cc = x.shape
    if T not in _NC_CACHE:
        _NC_CACHE[T] = build_nc(T)
    nc = _NC_CACHE[T]

    def f32(a):
        return np.ascontiguousarray(np.asarray(a, dtype=np.float32))

    shared = {
        "Wk": f32(L0_attn_Wk), "Wv": f32(L0_attn_Wv), "Wr": f32(L0_attn_Wr),
        "Wo": f32(L0_attn_Wo), "td": f32(L0_attn_time_decay),
        "tf": f32(L0_attn_time_first), "tmk": f32(L0_attn_time_mix_k),
        "tmv": f32(L0_attn_time_mix_v), "tmr": f32(L0_attn_time_mix_r),
        "wg": f32(L0_ffn_w_gate), "wu": f32(L0_ffn_w_up), "wd": f32(L0_ffn_w_down),
        "clipx": f32(L0_ffn_clip_x), "clipd": f32(L0_ffn_clip_down),
    }
    in_maps = [dict(shared, x=x[c]) for c in range(Bn)]
    res = run_bass_kernel_spmd(nc, in_maps, core_ids=list(range(Bn)))
    out = np.stack([np.asarray(res.results[c]["out"]) for c in range(Bn)], axis=0)
    return out.astype(np.float32)
